# version 11
# baseline (speedup 1.0000x reference)
import numpy as np
import ml_dtypes
import concourse.bass as bass
import concourse.mybir as mybir
from concourse.bass_utils import run_bass_kernel_spmd

F32 = mybir.dt.float32
BF16 = mybir.dt.bfloat16
I32 = mybir.dt.int32
AF = mybir.ActivationFunctionType
ALU = mybir.AluOpType
AX = mybir.AxisListType

P = 128
D = 1024
H = 16
DH = 64
FF = 4096
DC = D // P
FC = FF // P
NW = 512
TPC = 4096
NT = TPC // P
EPS_LN = 1e-5
EPS_DEN = 1e-6
MAGIC = float(1.5 * 2**23)

PHASE = "ABC"

ENGINES = ("pe", "act", "dve", "pool", "sp")
DMA_SEMS = ("sxa", "swl", "sq", "sqs", "sxb", "ssp", "sxc", "stw",
            "sout", "sex", "sexp", "sexr")


class Emitter:
    def __init__(self, nc, engines, sems, active):
        self.nc = nc
        self.engines = engines
        self.sems = sems
        self.active = active
        self.counters = {name: 0 for name in list(ENGINES) + list(DMA_SEMS)}
        self.events = {}
        self.waited = {}

    def op(self, key, fn, inc=False, sem=None):
        semname = sem or key
        amt = 16 if semname in DMA_SEMS else 1
        if key == self.active:
            inst = fn(self.engines[key])
            if inc and inst is not None:
                if semname in DMA_SEMS or key == "pe":
                    inst.then_inc(self.sems[semname], amt)
                else:
                    self.engines[key].maybe_drain_then_inc(
                        (self.sems[semname], amt))
        if inc:
            self.counters[semname] += amt

    def mark(self, name, semname):
        self.events[name] = (semname, self.counters[semname])

    def wait(self, key, event):
        semname, val = self.events[event]
        if val <= 0:
            return
        prev = self.waited.get((key, semname), -1)
        if val <= prev:
            return
        self.waited[(key, semname)] = val
        if key == self.active:
            self.engines[key].wait_ge(self.sems[semname], val)

    def raw(self, key, fn):
        if key == self.active:
            return fn(self.engines[key])
        return None


def build_nc():
    nc = bass.Bass("TRN2", target_bir_lowering=False, debug=False)

    xo = nc.dram_tensor("xo", [TPC, D], F32, kind="ExternalInput").ap()
    xf = nc.dram_tensor("xf", [TPC, D], F32, kind="ExternalInput").ap()
    wqd = nc.dram_tensor("wq", [D, D], F32, kind="ExternalInput").ap()
    wkd = nc.dram_tensor("wk", [D, D], F32, kind="ExternalInput").ap()
    wvd = nc.dram_tensor("wv", [D, D], F32, kind="ExternalInput").ap()
    wod = nc.dram_tensor("wo", [D, D], F32, kind="ExternalInput").ap()
    w1d = nc.dram_tensor("w1", [FF, D], F32, kind="ExternalInput").ap()
    w2d = nc.dram_tensor("w2", [D, FF], F32, kind="ExternalInput").ap()
    identd = nc.dram_tensor("identd", [P, P], BF16, kind="ExternalInput").ap()
    g1cols = nc.dram_tensor("g1cols", [P, DC], F32, kind="ExternalInput").ap()
    b1lncols = nc.dram_tensor("b1lncols", [P, DC], F32, kind="ExternalInput").ap()
    g2cols = nc.dram_tensor("g2cols", [P, FC], F32, kind="ExternalInput").ap()
    b2lncols = nc.dram_tensor("b2lncols", [P, FC], F32, kind="ExternalInput").ap()
    b1row = nc.dram_tensor("b1row", [1, FF], F32, kind="ExternalInput").ap()
    b2row = nc.dram_tensor("b2row", [1, D], F32, kind="ExternalInput").ap()
    meta = nc.dram_tensor("meta", [1, 8], I32, kind="ExternalInput").ap()

    y = nc.dram_tensor("y", [TPC, D], F32, kind="ExternalOutput").ap()

    x1d = nc.dram_tensor("x1d", [TPC, D], F32, kind="Internal").ap()
    w1qd = nc.dram_tensor("w1qd", [FF, D], BF16, kind="Internal").ap()
    w2qd = nc.dram_tensor("w2qd", [D, FF], BF16, kind="Internal").ap()
    shdata = nc.dram_tensor("shdata", [2 * P, 520], F32, kind="Internal",
                            addr_space="Shared").ap()
    shflag = nc.dram_tensor("shflag", [1, 32], I32, kind="Internal",
                            addr_space="Shared").ap()

    SB_BASE = 16512
    SB_TOP = 229344
    ptr = [SB_BASE]

    sb_off = {}

    def sb(name, shape, dt, align=32, at=None):
        nbytes = int(np.prod(shape[1:])) * mybir.dt.size(dt)
        if at is None:
            off = (ptr[0] + align - 1) // align * align
            ptr[0] = off + nbytes
        else:
            off = at
        sb_off[name] = off
        assert max(ptr[0], off + nbytes) <= SB_TOP, f"SBUF overflow at {name}"
        return nc.alloc_sbuf_tensor_at(name, shape, dt, offset=off)

    ident = sb("ident", [P, P], BF16)
    onesrow = sb("onesrow", [1, P], BF16)
    onesrow_f = sb("onesrow_f", [1, P], F32)
    onescol_f = sb("onescol_f", [P, 1], F32)
    onescol_bf = sb("onescol_bf", [P, 1], BF16)
    mt = sb("mt", [1, 8], I32)
    flagrd = sb("flagrd", [1, 32], I32)
    g1c = sb("g1c", [P, DC], F32)
    b1lc = sb("b1lc", [P, DC], F32)
    g2c = sb("g2c", [P, FC], F32)
    b2lc = sb("b2lc", [P, FC], F32)
    biasld0 = sb("biasld0", [1, NW], F32)
    biasld1 = sb("biasld1", [1, NW], F32)
    bias_tmp = sb("bias_tmp", [1, NW], F32)
    b1lc_bf = sb("b1lc_bf", [P, DC], BF16)
    b2lc_bf = sb("b2lc_bf", [P, FC], BF16)
    absacc = sb("absacc", [P, 66], F32)
    scl = sb("scl", [1, 8], F32)
    sclb = sb("sclb", [P, 4], F32)
    st6 = sb("st6", [P, 12], F32)
    st6p = sb("st6p", [P, 12], F32)
    mvp = sb("mvp", [P, 2], F32)
    nmp = sb("nmp", [P, 1], F32)
    rsp = sb("rsp", [P, 1], F32)
    rt1p = sb("rt1p", [P, 1], F32)
    rt2p = sb("rt2p", [P, 1], F32)
    st6b = sb("st6b", [P, 48], F32)
    mv = sb("mv", [P, 2], F32)
    nm = sb("nm", [P, 1], F32)
    rs = sb("rs", [P, 1], F32)
    rt1 = sb("rt1", [P, 1], F32)
    rt2 = sb("rt2", [P, 1], F32)
    qred = sb("qred", [P, 1], F32)
    kv_mine = sb("kv_mine", [P, 520], F32)
    kv_part = sb("kv_part", [P, 520], F32)
    kv_ext = sb("kv_ext", [P, 520], BF16)
    recip = sb("recip", [P, H], F32)
    den_s = sb("den_s", [P, H], F32)
    ones2 = sb("ones2", [P, P], BF16)
    bias1m = sb("bias1m", [P, FF], BF16)
    bias2m = sb("bias2m", [P, D], BF16)

    DYN = (ptr[0] + 255) // 256 * 256

    ptr[0] = DYN
    wq_s = sb("wq_s", [P, DC * D], BF16)
    wk_s = sb("wk_s", [P, DC * D], BF16)
    wv_s = sb("wv_s", [P, DC * D], BF16)
    wo_s = sb("wo_s", [P, DC * D], BF16)
    stash = sb("stash", [P, NT * D], BF16)
    xb0 = sb("xb0", [P, D], F32)
    xb1 = sb("xb1", [P, D], F32)
    lnb0 = sb("lnb0", [P, D], BF16)
    lnb1 = sb("lnb1", [P, D], BF16)
    lnT0 = sb("lnT0", [P, D], BF16)
    lnT1 = sb("lnT1", [P, D], BF16)
    kf0 = sb("kf0", [P, D], BF16)
    kf1 = sb("kf1", [P, D], BF16)
    v0 = sb("v0", [P, D], BF16)
    v1 = sb("v1", [P, D], BF16)
    qf_b = sb("qf_b", [P, D], BF16)
    tmpf = sb("tmpf", [P, D], F32)
    tmpe = sb("tmpe", [P, D], F32)
    wt0 = sb("wt0", [P, D], F32)
    wt1 = sb("wt1", [P, D], F32)
    wqt0 = sb("wqt0", [P, D], BF16)
    wqt1 = sb("wqt1", [P, D], BF16)
    outb = sb("outb", [P, D], BF16)
    outT = sb("outT", [P, D], BF16)
    x1f = sb("x1f", [P, D], F32)
    wst0 = sb("wst0", [P, D], F32, at=sb_off["wt0"])
    wst1 = sb("wst1", [P, D], F32, at=sb_off["wt1"])
    wst2 = sb("wst2", [P, D], F32, at=sb_off["outb"])
    wst3 = sb("wst3", [P, D], F32, at=sb_off["x1f"])
    AB_END = ptr[0]

    ptr[0] = DYN
    w1T = sb("w1T", [P, DC * FF], BF16)
    w2T = sb("w2T", [P, FC * D], BF16)
    cx0 = sb("cx0", [P, D], F32)
    cx1 = sb("cx1", [P, D], F32)
    zb = sb("zb", [P, D], BF16)
    zT = sb("zT", [P, D], BF16, at=sb_off["zb"])
    hbuf = sb("hbuf", [P, FF], F32)
    z2b = sb("z2b", [P, FF], BF16)
    z2T = sb("z2T", [P, FF], BF16)
    outf = sb("outf", [P, D], F32)
    C_END = ptr[0]
    absrep = sb("absrep", [P, 256], F32, at=sb_off["tmpf"])
    bcr2 = sb("bcr2", [P, FF], BF16, at=sb_off["hbuf"])
    bcr1 = sb("bcr1", [P, D], BF16, at=sb_off["hbuf"] + 8192)

    pT0 = nc.alloc_psum_tensor("pT0", [P, 1024], BF16)
    pT1 = nc.alloc_psum_tensor("pT1", [P, 1024], BF16)
    pA0 = nc.alloc_psum_tensor("pA0", [P, NW], F32)
    pA1 = nc.alloc_psum_tensor("pA1", [P, NW], F32)
    pA2 = nc.alloc_psum_tensor("pA2", [P, NW], F32)
    pA3 = nc.alloc_psum_tensor("pA3", [P, NW], F32)
    pKv = nc.alloc_psum_tensor("pKv", [P, NW], F32)
    pK1 = nc.alloc_psum_tensor("pK1", [P, H], F32)

    from contextlib import ExitStack
    stack = ExitStack()
    sems = {}
    for name in list(ENGINES) + list(DMA_SEMS):
        sems[name] = stack.enter_context(nc.semaphore(name))
    block = stack.enter_context(nc.Block())

    def a_prev(tag, i, k=1):
        g = (i if tag == "ao" else NT + i) - k
        if g < 0:
            return None
        return ("ao", g) if g < NT else ("af", g - NT)

    def emit_program(E):
        ev, w, op = E.mark, E.wait, E.op
        fence_n = [0]

        def dve_fence():
            name = f"fence.{fence_n[0]}"
            fence_n[0] += 1
            ev(name, "dve")
            w("dve", name)

        def pe_drain():
            op("pe", lambda e: e.drain())

        op("sp", lambda e: e.dma_start(out=ident[:], in_=identd[:, :]), inc=True, sem="swl")
        op("sp", lambda e: e.dma_start(out=mt[:], in_=meta[:, :]), inc=True, sem="swl")
        for src, dst in [(g1cols, g1c), (b1lncols, b1lc), (g2cols, g2c),
                         (b2lncols, b2lc)]:
            op("sp", lambda e, s=src, d=dst: e.dma_start(out=d[:], in_=s[:, :]),
               inc=True, sem="swl")
        ev("init.loads", "swl")

        wcnt = 0
        for wsrc, wdst, tag in [(wqd, wq_s, "wq"), (wkd, wk_s, "wk"),
                                (wvd, wv_s, "wv"), (wod, wo_s, "wo")]:
            for j in range(DC):
                cnt = wcnt
                wcnt += 1
                wst = (wst0, wst1, wst2, wst3)[cnt % 4]
                if cnt >= 4:
                    w("sp", f"wl.cast.{cnt - 4}")
                op("sp", lambda e, s=wsrc, ws=wst, jj=j:
                   e.dma_start(out=ws[:], in_=s[jj * P:(jj + 1) * P, :]),
                   inc=True, sem="swl")
                ev(f"wl.ld.{cnt}", "swl")
                w("dve", f"wl.ld.{cnt}")
                op("dve", lambda e, d=wdst, ws=wst, jj=j:
                   e.tensor_copy(out=d[:, jj * D:(jj + 1) * D], in_=ws[:]),
                   inc=True)
                ev(f"wl.cast.{cnt}", "dve")
            ev(f"wl.{tag}", "dve")

        w("dve", "init.loads")
        op("dve", lambda e: e.memset(onesrow[:], 1.0), inc=True)
        op("dve", lambda e: e.memset(onesrow_f[:], 1.0), inc=True)
        op("dve", lambda e: e.memset(onescol_f[:], 1.0), inc=True)
        op("dve", lambda e: e.memset(onescol_bf[:], 1.0), inc=True)
        op("dve", lambda e: e.memset(absacc[:], 0.0), inc=True)
        op("dve", lambda e: e.tensor_copy(out=b1lc_bf[:], in_=b1lc[:]), inc=True)
        op("dve", lambda e: e.tensor_copy(out=b2lc_bf[:], in_=b2lc[:]), inc=True)
        op("dve", lambda e: e.memset(ones2[:], 0.0), inc=True)
        op("dve", lambda e: e.memset(ones2[0:1, :], 1.0), inc=True)
        op("dve", lambda e: e.memset(ones2[32:33, :], 1.0), inc=True)
        op("dve", lambda e: e.memset(bias1m[:], 0.0), inc=True)
        op("dve", lambda e: e.memset(bias2m[:], 0.0), inc=True)
        ev("init.dve", "dve")

        scr0 = dict(st6=st6, mv=mv, nm=nm, rs=rs, rt1=rt1, rt2=rt2)
        scr1 = dict(st6=st6p, mv=mvp, nm=nmp, rs=rsp, rt1=rt1p, rt2=rt2p)

        def newton_rsqrt(var_ap, eps_add, var_scale, S=scr0):
            nme = f"rsq.{fence_n[0]}"
            fence_n[0] += 1
            op("dve", lambda e: e.tensor_scalar(out=S["rt1"][:], in0=var_ap,
                                                scalar1=float(var_scale),
                                                scalar2=float(eps_add),
                                                op0=ALU.mult, op1=ALU.add), inc=True)
            ev(f"{nme}.a", "dve")
            w("act", f"{nme}.a")
            op("act", lambda e: e.activation(out=S["rt2"][:], in_=S["rt1"][:],
                                             func=AF.Ln), inc=True)
            ev(f"{nme}.ln", "act")
            w("act", f"{nme}.ln")
            op("act", lambda e: e.activation(out=S["rs"][:], in_=S["rt2"][:],
                                             func=AF.Exp, scale=-0.5), inc=True)
            ev(f"{nme}.b", "act")
            w("dve", f"{nme}.b")
            return f"{nme}.b"

        def ln_stats(xsb, eps_add, var_scale, S=scr0):
            op("dve", lambda e: e.bn_stats(out=S["st6"][:, 0:6], in_=xsb[:, 0:512]), inc=True)
            op("dve", lambda e: e.bn_stats(out=S["st6"][:, 6:12], in_=xsb[:, 512:1024]), inc=True)
            dve_fence()
            op("dve", lambda e: e.bn_aggr(out=S["mv"][:, 0:2], in_=S["st6"][:, 0:12]), inc=True)
            dve_fence()
            op("dve", lambda e: e.tensor_scalar(out=S["nm"][:], in0=S["mv"][:, 0:1],
                                                scalar1=-1.0, scalar2=None,
                                                op0=ALU.mult), inc=True)
            return newton_rsqrt(S["mv"][:, 1:2], eps_add, var_scale, S)
        q_tiles = [("w1", i) for i in range(FC)] + [("w2", i, j) for i in range(DC)
                                                   for j in range(4)]

        def q_src_ap(t):
            if t[0] == "w1":
                return w1d[t[1] * P:(t[1] + 1) * P, :]
            _, i, j = t
            return w2d[i * P:(i + 1) * P, j * D:(j + 1) * D]

        def q_dst_ap(t):
            if t[0] == "w1":
                return w1qd[t[1] * P:(t[1] + 1) * P, :]
            _, i, j = t
            return w2qd[i * P:(i + 1) * P, j * D:(j + 1) * D]

        q1_idx = [0]

        def q1_step():
            idx = q1_idx[0]
            if idx >= len(q_tiles):
                return
            q1_idx[0] += 1
            t = q_tiles[idx]
            buf = wt0 if idx % 2 == 0 else wt1
            col = 0 if t[0] == "w1" else 1
            if idx >= 2:
                w("sp", f"q1.red.{idx - 2}")
            else:
                w("sp", f"wl.cast.{28 + idx}")
            op("sp", lambda e: e.dma_start(out=buf[:], in_=q_src_ap(t)),
               inc=True, sem="sq")
            ev(f"q1.load.{idx}", "sq")
            w("dve", f"q1.load.{idx}")
            op("dve", lambda e, bb=buf: e.tensor_scalar(
                out=bb[:].bitcast(I32), in0=bb[:].bitcast(I32),
                scalar1=0x7FFFFFFF, scalar2=None,
                op0=ALU.bitwise_and), inc=True)
            op("dve", lambda e, ii=idx, bb=buf: e.tensor_scalar(
                out=bb[:], in0=bb[:], scalar1=0.0, scalar2=None,
                op0=ALU.add, op1=ALU.add,
                accum_out=absacc[:, 2 + ii:3 + ii]), inc=True)
            ev(f"q1.red.{idx}", "dve")

        state = {"prev_exp": None}
        bank_last = {0: None, 1: None}
        last_rsq = {0: None, 1: None}

        def elu_evac(tag_ev, pb0, pb1, dstbf):
            if state["prev_exp"] is not None:
                w("dve", state["prev_exp"])
            for half in range(2):
                pb = pb0 if half == 0 else pb1
                s = half * NW
                op("dve", lambda e, pbb=pb, ss=s: e.tensor_scalar(
                    out=tmpf[:, ss:ss + NW], in0=pbb[:, 0:NW],
                    scalar1=0.0, scalar2=None, op0=ALU.min), inc=(half == 1))
            ev(f"{tag_ev}.min", "dve")
            w("act", f"{tag_ev}.min")
            op("act", lambda e: e.activation(out=tmpe[:], in_=tmpf[:],
                                             func=AF.Exp), inc=True)
            ev(f"{tag_ev}.exp", "act")
            state["prev_exp"] = f"{tag_ev}.exp"
            w("dve", f"{tag_ev}.exp")
            for half in range(2):
                pb = pb0 if half == 0 else pb1
                s = half * NW
                op("dve", lambda e, pbb=pb, ss=s: e.scalar_tensor_tensor(
                    out=dstbf[:, ss:ss + NW], in0=pbb[:, 0:NW], scalar=0.0,
                    in1=tmpe[:, ss:ss + NW], op0=ALU.max, op1=ALU.add),
                    inc=(half == 1))
            ev(tag_ev, "dve")

        def transpose_group(srcbf, bank_id, evname, dst_sb, evac_evname):
            pb = pT0 if bank_id == 0 else pT1
            if bank_last[bank_id] is not None:
                w("pe", bank_last[bank_id])
            pe_drain()
            for j in range(DC):
                op("pe", lambda e, jj=j: e.transpose(
                    out=pb[:, jj * P:(jj + 1) * P],
                    in_=srcbf[:, jj * P:(jj + 1) * P],
                    identity=ident[:]), inc=(j == DC - 1))
            pe_drain()
            ev(evname, "pe")
            w("act", evname)
            op("act", lambda e: e.copy(out=dst_sb, in_=pb[:]), inc=True)
            ev(evac_evname, "act")
            bank_last[bank_id] = evac_evname

        def mm_proj(lnTb, wsb, pb0, pb1, evname, wl_ev):
            w("pe", wl_ev)
            for j in range(DC):
                for half in range(2):
                    pb = pb0 if half == 0 else pb1
                    op("pe", lambda e, jj=j, hh=half, pbb=pb: e.matmul(
                        out=pbb[:, 0:NW],
                        lhsT=lnTb[:, jj * P:(jj + 1) * P],
                        rhs=wsb[:, jj * D + hh * NW: jj * D + hh * NW + NW],
                        start=(jj == 0), stop=(jj == DC - 1)),
                        inc=(j == DC - 1 and half == 1))
            ev(evname, "pe")

        def phase_a_tile(tag, i):
            foreign = tag == "af"
            g = i if tag == "ao" else NT + i
            p = g % 2
            xbuf = xb0 if p == 0 else xb1
            lnb = lnb0 if p == 0 else lnb1
            lnT = lnT0 if p == 0 else lnT1
            kf_b = kf0 if p == 0 else kf1
            v_b = v0 if p == 0 else v1
            S = scr0 if p == 0 else scr1
            xsrc = xf if foreign else xo

            p1 = a_prev(tag, i, 1)
            p2 = a_prev(tag, i, 2)

            if p2 is not None:
                w("sp", f"{p2[0]}.lnb.{p2[1]}")
            op("sp", lambda e: e.dma_start(out=xbuf[:],
                                           in_=xsrc[i * P:(i + 1) * P, :]),
               inc=True, sem="sxa")
            ev(f"{tag}.x.{i}", "sxa")

            w("dve", f"{tag}.x.{i}")
            if p2 is not None:
                w("dve", f"{p2[0]}.T.{p2[1]}")
            if last_rsq[p] is not None:
                w("dve", last_rsq[p])
            rev = ln_stats(xbuf, EPS_LN, 1.0, S)
            last_rsq[p] = rev
            op("dve", lambda e, lb=lnb, xb=xbuf, SS=S: e.tensor_scalar(
                out=lb[:], in0=xb[:],
                scalar1=SS["nm"][:, 0:1], scalar2=SS["rs"][:, 0:1],
                op0=ALU.add, op1=ALU.mult), inc=True)
            ev(f"{tag}.lnb.{i}", "dve")

            w("pe", f"{tag}.lnb.{i}")
            if p2 is not None:
                w("act", f"{p2[0]}.V.{p2[1]}")
            transpose_group(lnb, p, f"{tag}.T.{i}", lnT[:], f"{tag}.lnT.evac.{i}")

            w("pe", f"{tag}.lnT.evac.{i}")

            if not foreign:
                if p1 is not None:
                    w("pe", f"{p1[0]}.v.evac.{p1[1]}")
                mm_proj(lnT, wq_s, pA0, pA1, f"{tag}.Q.{i}", "wl.wq")
                w("dve", f"{tag}.Q.{i}")
                if i > 0:
                    w("dve", f"ao.qfT.{i - 1}")
                elu_evac(f"ao.qf.{i}", pA0, pA1, qf_b)
                w("pe", f"ao.qf.{i}")
                transpose_group(qf_b, 1 - p, f"ao.qfT.{i}",
                                stash[:, i * D:(i + 1) * D], f"ao.qfT.evac.{i}")

            if p1 is not None:
                w("pe", f"{p1[0]}.kf.{p1[1]}")
            mm_proj(lnT, wk_s, pA2, pA3, f"{tag}.K.{i}", "wl.wk")
            w("dve", f"{tag}.K.{i}")
            if p2 is not None:
                w("dve", f"{p2[0]}.kv.{p2[1]}")
            elu_evac(f"{tag}.kf.{i}", pA2, pA3, kf_b)

            if not foreign:
                w("pe", f"ao.qf.{i}")
            elif p1 is not None:
                w("pe", f"{p1[0]}.v.evac.{p1[1]}")
            mm_proj(lnT, wv_s, pA0, pA1, f"{tag}.V.{i}", "wl.wv")
            w("dve", f"{tag}.V.{i}")
            if p2 is not None:
                w("dve", f"{p2[0]}.kv.{p2[1]}")
            for half in range(2):
                pb = pA0 if half == 0 else pA1
                s = half * NW
                op("dve", lambda e, pbb=pb, ss=s, vb=v_b: e.tensor_copy(
                    out=vb[:, ss:ss + NW], in_=pbb[:, 0:NW]), inc=(half == 1))
            ev(f"{tag}.v.evac.{i}", "dve")

            w("pe", f"{tag}.kf.{i}")
            w("pe", f"{tag}.v.evac.{i}")
            first = (tag == "ao" and i == 0)
            for h in range(H):
                pb = 64 * (h % 2)
                cb = 64 * (h // 2)
                st = first and h < 2
                op("pe", lambda e, hh=h, pbb=pb, cbb=cb, stt=st, kb=kf_b, vb=v_b: e.matmul(
                    out=pKv[pbb:pbb + 64, cbb:cbb + 64],
                    lhsT=kb[:, hh * DH:(hh + 1) * DH],
                    rhs=vb[:, hh * DH:(hh + 1) * DH],
                    start=stt, stop=False, skip_group_check=True))
                op("pe", lambda e, hh=h, pbb=pb, stt=st, kb=kf_b: e.matmul(
                    out=pK1[pbb:pbb + 64, hh // 2:hh // 2 + 1],
                    lhsT=kb[:, hh * DH:(hh + 1) * DH],
                    rhs=onescol_bf[:, 0:1],
                    start=stt, stop=False, skip_group_check=True),
                    inc=(h == H - 1))
            ev(f"{tag}.kv.{i}", "pe")

        w("dve", "init.dve")
        for i in range(NT):
            phase_a_tile("ao", i)
            q1_step()
        for i in range(NT):
            phase_a_tile("af", i)
            q1_step()
        while q1_idx[0] < len(q_tiles):
            q1_step()

        w("dve", f"af.kv.{NT - 1}")
        op("dve", lambda e: e.tensor_copy(out=kv_mine[:, 0:512], in_=pKv[:, 0:NW]), inc=True)
        op("dve", lambda e: e.tensor_copy(out=kv_mine[:, 512:520], in_=pK1[:, 0:8]), inc=True)
        ev("kv.mine", "dve")

        w("dve", f"q1.red.{len(q_tiles) - 1}")
        dve_fence()
        op("dve", lambda e: e.tensor_reduce(out=absacc[:, 0:1],
                                            in_=absacc[:, 2:2 + FC],
                                            axis=AX.X, op=ALU.add), inc=True)
        op("dve", lambda e: e.tensor_reduce(out=absacc[:, 1:2],
                                            in_=absacc[:, 2 + FC:2 + 2 * FC],
                                            axis=AX.X, op=ALU.add), inc=True)
        dve_fence()
        for col in range(2):
            op("dve", lambda e, cc=col: e.tensor_scalar(
                out=absrep[:, cc * P:(cc + 1) * P], in0=xb0[:, 0:P],
                scalar1=0.0, scalar2=absacc[:, cc:cc + 1],
                op0=ALU.mult, op1=ALU.add), inc=(col == 1))
        ev("q.absrep", "dve")
        w("pe", "q.absrep")
        w("pe", f"af.v.evac.{NT - 1}")
        w("pe", f"af.kf.{NT - 1}")
        for col in range(2):
            op("pe", lambda e, cc=col: e.matmul(
                out=pA0[:, cc:cc + 1],
                lhsT=absrep[:, cc * P:(cc + 1) * P],
                rhs=onescol_f[:, 0:1],
                start=(cc == 0), stop=(cc == 1), skip_group_check=True),
                inc=(col == 1))
        ev("q.sums", "pe")
        w("dve", "q.sums")
        op("dve", lambda e: e.tensor_scalar(out=sclb[:, 2:3], in0=pA0[:, 0:1],
                                            scalar1=float(1.0 / (FF * D)),
                                            scalar2=1e-5, op0=ALU.mult,
                                            op1=ALU.max), inc=True)
        op("dve", lambda e: e.tensor_scalar(out=sclb[:, 3:4], in0=pA0[:, 1:2],
                                            scalar1=float(1.0 / (D * FF)),
                                            scalar2=1e-5, op0=ALU.mult,
                                            op1=ALU.max), inc=True)
        op("dve", lambda e: e.reciprocal(out=sclb[:, 0:1], in_=sclb[:, 2:3]), inc=True)
        op("dve", lambda e: e.reciprocal(out=sclb[:, 1:2], in_=sclb[:, 3:4]), inc=True)
        ev("q.sclb", "dve")

        def exchange(g):
            slot = g.value_load(mt[0:1, 0:1])
            g.dma_start(out=shdata[bass.ts(slot, P), 0:520],
                        in_=kv_mine[:]).then_inc(sems["sex"], 16)
            g.wait_ge(sems["sex"], 16)
            g.dma_start(out=shflag[0:1, bass.ts(slot, 16)][0:1, 0:1],
                        in_=mt[0:1, 1:2]).then_inc(sems["sex"], 16)
            g.wait_ge(sems["sex"], 32)
            nonce = g.value_load(mt[0:1, 1:2])
            with g.register("wp") as wp, g.register("cr") as cr:
                g.reg_mov(wp, 0)
                g.reg_mov(cr, 1)
                with g.While(cr):
                    g.dma_start(out=flagrd[0:1, 0:32],
                                in_=shflag[0:1, 0:32]).then_inc(sems["sexp"], 16)
                    g.reg_alu(wp, wp, 16, ALU.add)
                    g.wait_ge(sems["sexp"], wp)
                    f0 = g.value_load(flagrd[0:1, 0:1])
                    f1 = g.value_load(flagrd[0:1, 16:17])
                    e0 = g.compute_val(f0 == nonce)
                    e1 = g.compute_val(f1 == nonce)
                    sm = g.compute_val(e0 + e1)
                    dn = g.compute_val(sm != 2)
                    g.reg_mov(cr, dn)
                    g.nop(cycle_cnt=250)
            prt = g.compute_val(slot == 0)
            g.dma_start(out=kv_part[:], in_=shdata[bass.ts(prt, P), 0:520]
                        ).then_inc(sems["sexr"], 16)
            g.wait_ge(sems["sexr"], 16)

        w("pool", "kv.mine")
        w("pool", "init.loads")
        E.raw("pool", exchange)
        op("pool", lambda e: e.nop(), inc=True)
        ev("kv.exchanged", "pool")

        w("dve", "kv.exchanged")
        op("dve", lambda e: e.tensor_tensor(out=kv_mine[:], in0=kv_mine[:],
                                            in1=kv_part[:], op=ALU.add), inc=True)
        for h in range(H):
            pb = 64 * (h % 2)
            op("dve", lambda e, hh=h, pbb=pb: e.tensor_copy(
                out=kv_ext[pbb:pbb + 64, 65 * (hh // 2):65 * (hh // 2) + 64],
                in_=kv_mine[pbb:pbb + 64, 64 * (hh // 2):64 * (hh // 2) + 64]),
                inc=(h == H - 1))
        for h in range(H):
            pb = 64 * (h % 2)
            op("dve", lambda e, hh=h, pbb=pb: e.tensor_copy(
                out=kv_ext[pbb:pbb + 64, 65 * (hh // 2) + 64:65 * (hh // 2) + 65],
                in_=kv_mine[pbb:pbb + 64, 512 + hh // 2:512 + hh // 2 + 1]),
                inc=(h == H - 1))
        ev("kv.ext", "dve")

        if PHASE == "A":
            w("sp", "kv.ext")
            op("sp", lambda e: e.dma_start(out=y[0:P, 0:520], in_=kv_mine[:]),
               inc=True, sem="sout")
            w("pool", "kv.ext")
            op("pool", lambda e: e.dma_start(out=y[P:2 * P, :], in_=lnb[:]),
               inc=True, sem="sout")
            op("pool", lambda e: e.dma_start(out=y[2 * P:3 * P, :], in_=kf_b[:]),
               inc=True, sem="sout")
            op("pool", lambda e: e.dma_start(out=y[3 * P:4 * P, :], in_=v_b[:]),
               inc=True, sem="sout")
            op("pool", lambda e: e.dma_start(allow_slow_non_contiguous=True, out=y[0:P, 520:522], in_=mv[:]),
               inc=True, sem="sout")
            op("pool", lambda e: e.dma_start(allow_slow_non_contiguous=True, out=y[0:P, 522:523], in_=nm[:]),
               inc=True, sem="sout")
            op("pool", lambda e: e.dma_start(allow_slow_non_contiguous=True, out=y[0:P, 523:524], in_=rs[:]),
               inc=True, sem="sout")
            op("pool", lambda e: e.dma_start(allow_slow_non_contiguous=True, out=y[0:P, 524:525], in_=rt1[:]),
               inc=True, sem="sout")
            ev("dbg.y", "sout")
            w("sp", "dbg.y")
            return

        NQ = len(q_tiles)
        w("pool", "q.sclb")
        for idx, t in enumerate(q_tiles):
            buf = wt0 if idx % 2 == 0 else wt1
            qbuf = wqt0 if idx % 2 == 0 else wqt1
            scol = 0 if t[0] == "w1" else 1
            if idx >= 2:
                w("sp", f"q2.quant.{idx - 2}")
                w("sp", f"q2.store.{idx - 2}")
            else:
                w("sp", f"q1.red.{NQ - 2 + idx}")
            op("sp", lambda e, tt=t, bb=buf: e.dma_start(out=bb[:], in_=q_src_ap(tt)),
               inc=True, sem="sq")
            ev(f"q2.load.{idx}", "sq")
            w("pool", f"q2.load.{idx}")
            if idx >= 2:
                w("pool", f"q2.store.{idx - 2}")
            op("pool", lambda e, bb=buf, sc=scol: e.tensor_scalar(
                out=bb[:], in0=bb[:], scalar1=sclb[:, sc:sc + 1],
                scalar2=MAGIC, op0=ALU.mult, op1=ALU.add), inc=True)
            op("pool", lambda e, bb=buf: e.tensor_scalar(
                out=bb[:], in0=bb[:], scalar1=-MAGIC, scalar2=1.0,
                op0=ALU.add, op1=ALU.mult), inc=True)
            op("pool", lambda e, bb=buf, qb=qbuf: e.tensor_scalar(
                out=qb[:], in0=bb[:], scalar1=1.0, scalar2=-1.0,
                op0=ALU.min, op1=ALU.max), inc=True)
            ev(f"q2.quant.{idx}", "pool")
            op("pool", lambda e, tt=t, qb=qbuf: e.dma_start(out=q_dst_ap(tt), in_=qb[:]),
               inc=True, sem="sqs")
            ev(f"q2.store.{idx}", "sqs")

        def phase_b_tile(i):
            xbuf = xb0 if i % 2 == 0 else xb1
            if i >= 2:
                w("sp", f"b.res.{i - 2}")
            else:
                w("sp", f"af.lnb.{NT - 2 + i}")
            op("sp", lambda e: e.dma_start(out=xbuf[:],
                                           in_=xo[i * P:(i + 1) * P, :]),
               inc=True, sem="sxb")
            ev(f"b.x.{i}", "sxb")

            if i == 0:
                w("pe", "kv.ext")
                w("pe", "q.sclb")
            else:
                w("pe", f"b.div.dve.{i - 1}")
                w("pe", f"b.div.act.{i - 1}")
            for h in range(H):
                pb = pA0 if h % 2 == 0 else pA1
                pd = pK1 if h % 2 == 0 else pKv
                cb = 64 * (h // 2)
                op("pe", lambda e, hh=h, pbb=pb, cbb=cb: e.matmul(
                    out=pbb[:, cbb:cbb + 64],
                    lhsT=stash[64 * (hh % 2):64 * (hh % 2) + 64,
                               i * D + (hh // 2) * P:i * D + (hh // 2) * P + P],
                    rhs=kv_ext[64 * (hh % 2):64 * (hh % 2) + 64,
                               65 * (hh // 2):65 * (hh // 2) + 64],
                    start=(hh < 2), stop=False, skip_group_check=True))
                op("pe", lambda e, hh=h, pdd=pd: e.matmul(
                    out=pdd[:, hh // 2:hh // 2 + 1],
                    lhsT=stash[64 * (hh % 2):64 * (hh % 2) + 64,
                               i * D + (hh // 2) * P:i * D + (hh // 2) * P + P],
                    rhs=kv_ext[64 * (hh % 2):64 * (hh % 2) + 64,
                               65 * (hh // 2) + 64:65 * (hh // 2) + 65],
                    start=(hh < 2), stop=True, skip_group_check=True),
                    inc=(h == H - 1))
            ev(f"b.num.{i}", "pe")

            w("dve", f"b.num.{i}")
            if i > 0:
                w("dve", f"b.div.act.{i - 1}")
            op("dve", lambda e: e.tensor_scalar(out=den_s[:, 0:8], in0=pK1[:, 0:8],
                                                scalar1=EPS_DEN, scalar2=None,
                                                op0=ALU.add), inc=True)
            op("dve", lambda e: e.tensor_scalar(out=den_s[:, 8:16], in0=pKv[:, 0:8],
                                                scalar1=EPS_DEN, scalar2=None,
                                                op0=ALU.add), inc=True)
            op("dve", lambda e: e.reciprocal(out=recip[:], in_=den_s[:]), inc=True)
            ev(f"b.recip.{i}", "dve")

            w("dve", f"b.recip.{i}")
            w("act", f"b.recip.{i}")
            if i > 0:
                w("dve", f"b.outT.{i - 1}")
                w("act", f"b.outT.{i - 1}")
            for h in range(H):
                pb = pA0 if h % 2 == 0 else pA1
                cb = 64 * (h // 2)
                rc = (h // 2) + 8 * (h % 2)
                if h % 2 == 0:
                    op("dve", lambda e, hh=h, pbb=pb, cbb=cb, rcc=rc: e.tensor_scalar(
                        out=outb[:, hh * DH:(hh + 1) * DH],
                        in0=pbb[:, cbb:cbb + 64],
                        scalar1=recip[:, rcc:rcc + 1], scalar2=None,
                        op0=ALU.mult), inc=(h == H - 2))
                else:
                    op("act", lambda e, hh=h, pbb=pb, cbb=cb, rcc=rc: e.activation(
                        out=outb[:, hh * DH:(hh + 1) * DH],
                        in_=pbb[:, cbb:cbb + 64], func=AF.Copy,
                        scale=recip[:, rcc:rcc + 1]), inc=(h == H - 1))
            ev(f"b.div.dve.{i}", "dve")
            ev(f"b.div.act.{i}", "act")

            w("pe", f"b.div.dve.{i}")
            w("pe", f"b.div.act.{i}")
            if i > 0:
                w("act", f"b.wo.{i - 1}")
            transpose_group(outb, i % 2, f"b.outT.{i}", outT[:],
                            f"b.outT.evac.{i}")

            w("pe", f"b.outT.evac.{i}")
            w("pe", "wl.wo")
            if i > 0:
                w("pe", f"b.res.{i - 1}")
            for j in range(DC):
                for half in range(2):
                    pb = pA2 if half == 0 else pA3
                    op("pe", lambda e, jj=j, hh=half, pbb=pb: e.matmul(
                        out=pbb[:, 0:NW],
                        lhsT=outT[:, jj * P:(jj + 1) * P],
                        rhs=wo_s[:, jj * D + hh * NW: jj * D + hh * NW + NW],
                        start=(jj == 0), stop=(jj == DC - 1)),
                        inc=(j == DC - 1 and half == 1))
            ev(f"b.wo.{i}", "pe")

            w("dve", f"b.wo.{i}")
            w("dve", f"b.x.{i}")
            if i > 0:
                w("dve", f"b.x1.{i - 1}")
            for half in range(2):
                pb = pA2 if half == 0 else pA3
                s = half * NW
                op("dve", lambda e, pbb=pb, ss=s: e.tensor_tensor(
                    out=x1f[:, ss:ss + NW], in0=pbb[:, 0:NW],
                    in1=xbuf[:, ss:ss + NW], op=ALU.add), inc=(half == 1))
            ev(f"b.res.{i}", "dve")
            w("sp", f"b.res.{i}")
            op("sp", lambda e: e.dma_start(out=x1d[i * P:(i + 1) * P, :],
                                           in_=x1f[:]), inc=True, sem="ssp")
            ev(f"b.x1.{i}", "ssp")

        for i in range(NT):
            phase_b_tile(i)

        if PHASE == "B":
            for i in range(NT):
                w("sp", f"b.x1.{i}")
                op("sp", lambda e, ii=i: e.dma_start(out=xb0[:], in_=x1d[ii * P:(ii + 1) * P, :]),
                   inc=True, sem="sxc")
                ev(f"dbg.l.{i}", "sxc")
                w("sp", f"dbg.l.{i}")
                op("sp", lambda e, ii=i: e.dma_start(out=y[ii * P:(ii + 1) * P, :], in_=xb0[:]),
                   inc=True, sem="sout")
                ev(f"dbg.s.{i}", "sout")
                w("sp", f"dbg.s.{i}")
            return

        w("act", f"b.wo.{NT - 1}")
        w("act", f"b.res.{NT - 1}")
        w("act", f"b.x1.{NT - 1}")
        w("act", "kv.exchanged")
        w("act", "q2.store.31")
        for j in range(DC):
            op("act", lambda e, jj=j: e.dma_start(
                out=w1T[:, jj * FF:(jj + 1) * FF],
                in_=w1qd[:, jj * P:(jj + 1) * P], transpose=True),
                inc=True, sem="stw")
        w("act", f"q2.store.{NQ - 1}")
        for j in range(FC):
            op("act", lambda e, jj=j: e.dma_start(
                out=w2T[:, jj * D:(jj + 1) * D],
                in_=w2qd[:, jj * P:(jj + 1) * P], transpose=True),
                inc=True, sem="stw")
        ev("c.wT", "stw")

        w("pe", "c.wT")
        w("pe", f"b.wo.{NT - 1}")
        w("dve", "kv.ext")
        for jj in range(DC):
            op("dve", lambda e, j_=jj: e.tensor_scalar(
                out=bcr1[:, j_ * P:(j_ + 1) * P], in0=ident[:, 0:P],
                scalar1=0.0, scalar2=b1lc[:, j_:j_ + 1],
                op0=ALU.mult, op1=ALU.add), inc=(jj == DC - 1))
        for jj in range(FC):
            op("dve", lambda e, j_=jj: e.tensor_scalar(
                out=bcr2[:, j_ * P:(j_ + 1) * P], in0=ident[:, 0:P],
                scalar1=0.0, scalar2=b2lc[:, j_:j_ + 1],
                op0=ALU.mult, op1=ALU.add), inc=(jj == FC - 1))
        ev("c.bcr", "dve")
        bias_jobs = [("b1", nj) for nj in range(8)] + [("b2", nj) for nj in range(2)]
        for bj, (kind, nj) in enumerate(bias_jobs):
            bld = biasld0 if bj % 2 == 0 else biasld1
            brow_src = b1row if kind == "b1" else b2row
            if bj >= 1:
                w("sp", f"c.bf.{bj - 1}")
            w("sp", f"b.x1.{NT - 1}")
            op("sp", lambda e, s=brow_src, bb=bld, nn=nj: e.dma_start(
                out=bb[0:1, 0:NW], in_=s[0:1, nn * NW:(nn + 1) * NW]),
                inc=True, sem="swl")
            ev(f"c.bld.{bj}", "swl")
            if bj > 0:
                w("pe", f"c.bf.{bj - 1}")
            else:
                w("pe", "c.bcr")
            if kind == "b1":
                for j in range(DC):
                    op("pe", lambda e, jj=j, nn=nj: e.matmul(
                        out=pA0[:, 0:NW],
                        lhsT=bcr1[:, jj * P:(jj + 1) * P],
                        rhs=w1T[:, jj * FF + nn * NW:jj * FF + nn * NW + NW],
                        start=(jj == 0), stop=(jj == DC - 1)),
                        inc=(j == DC - 1))
            else:
                for j in range(FC):
                    op("pe", lambda e, jj=j, nn=nj: e.matmul(
                        out=pA0[:, 0:NW],
                        lhsT=bcr2[:, jj * P:(jj + 1) * P],
                        rhs=w2T[:, jj * D + nn * NW:jj * D + nn * NW + NW],
                        start=(jj == 0), stop=(jj == FC - 1)),
                        inc=(j == FC - 1))
            ev(f"c.bmm.{bj}", "pe")
            w("dve", f"c.bmm.{bj}")
            w("dve", f"c.bld.{bj}")
            mcol = 2 if kind == "b1" else 3
            op("dve", lambda e, bb=bld, mc=mcol: e.scalar_tensor_tensor(
                out=bias_tmp[0:1, 0:NW], in0=pA0[0:1, 0:NW],
                scalar=sclb[0:1, mc:mc + 1], in1=bb[0:1, 0:NW],
                op0=ALU.mult, op1=ALU.add), inc=True)
            bm = bias1m if kind == "b1" else bias2m
            op("dve", lambda e, mm_=bm, nn=nj: e.tensor_copy(
                out=mm_[0:1, nn * NW:(nn + 1) * NW], in_=bias_tmp[0:1, 0:NW]), inc=True)
            op("dve", lambda e, mm_=bm, nn=nj: e.tensor_tensor(
                out=mm_[32:33, nn * NW:(nn + 1) * NW], in0=bias_tmp[0:1, 0:NW],
                in1=mm_[0:1, nn * NW:(nn + 1) * NW], op=ALU.subtract), inc=True)
            ev(f"c.bf.{bj}", "dve")
        op("dve", lambda e: e.tensor_scalar(out=g1c[:], in0=g1c[:],
                                            scalar1=sclb[:, 2:3], scalar2=None,
                                            op0=ALU.mult), inc=True)
        op("dve", lambda e: e.tensor_scalar(out=g2c[:], in0=g2c[:],
                                            scalar1=sclb[:, 3:4], scalar2=None,
                                            op0=ALU.mult), inc=True)
        for j in range(DC):
            op("dve", lambda e, jj=j: e.tensor_scalar(
                out=w1T[:, jj * FF:(jj + 1) * FF],
                in0=w1T[:, jj * FF:(jj + 1) * FF],
                scalar1=g1c[:, jj:jj + 1], scalar2=None, op0=ALU.mult),
                inc=(j == DC - 1))
        for j in range(FC):
            op("dve", lambda e, jj=j: e.tensor_scalar(
                out=w2T[:, jj * D:(jj + 1) * D],
                in0=w2T[:, jj * D:(jj + 1) * D],
                scalar1=g2c[:, jj:jj + 1], scalar2=None, op0=ALU.mult),
                inc=(j == FC - 1))
        ev("c.bias", "dve")

        def phase_c_tile(i):
            xbuf = cx0 if i % 2 == 0 else cx1
            if i >= 2:
                w("sp", f"c.out.{i - 2}")
            else:
                w("sp", f"b.res.{NT - 1}")
                w("sp", f"c.wT")
                w("sp", "c.bias")
            w("sp", f"b.x1.{i}")
            op("sp", lambda e: e.dma_start(out=xbuf[:],
                                           in_=x1d[i * P:(i + 1) * P, :]),
               inc=True, sem="sxc")
            ev(f"c.x.{i}", "sxc")

            w("dve", f"c.x.{i}")
            if i == 0:
                w("dve", "c.bias")
                w("dve", f"b.x1.{NT - 1}")
                w("dve", f"q2.store.{NQ - 1}")
            op("dve", lambda e: e.bn_stats(out=st6[:, 0:6], in_=xbuf[:, 0:512]), inc=True)
            op("dve", lambda e: e.bn_stats(out=st6[:, 6:12], in_=xbuf[:, 512:1024]), inc=True)
            dve_fence()
            op("dve", lambda e: e.bn_aggr(out=mv[:, 0:2], in_=st6[:, 0:12]), inc=True)
            dve_fence()
            op("dve", lambda e: e.tensor_scalar(out=nm[:], in0=mv[:, 0:1],
                                                scalar1=-1.0, scalar2=None,
                                                op0=ALU.mult), inc=True)
            newton_rsqrt(mv[:, 1:2], EPS_LN * EPS_LN, 1.0 + EPS_LN)
            if i > 0:
                w("dve", f"c.mm1.{i - 1}.7")
            op("dve", lambda e: e.tensor_scalar(out=zb[:], in0=xbuf[:],
                                                scalar1=nm[:, 0:1],
                                                scalar2=rs[:, 0:1],
                                                op0=ALU.add, op1=ALU.mult), inc=True)
            ev(f"c.z.{i}", "dve")

            w("pe", f"c.z.{i}")
            if i == 0:
                w("pe", "c.bias")
            if i > 0:
                w("act", f"c.mm1.{i - 1}.7")
            transpose_group(zb, i % 2, f"c.zT.{i}", zT[:], f"c.zT.evac.{i}")

            w("pe", f"c.zT.evac.{i}")
            for nj in range(8):
                pb = pA0 if nj % 2 == 0 else pA1
                if nj >= 2:
                    w("pe", f"c.gelu.{i}.{nj - 2}")
                elif i > 0:
                    w("pe", f"c.gelu.{i - 1}.{6 + nj}")
                for j in range(DC):
                    op("pe", lambda e, jj=j, nn=nj, pbb=pb: e.matmul(
                        out=pbb[:, 0:NW],
                        lhsT=zT[:, jj * P:(jj + 1) * P],
                        rhs=w1T[:, jj * FF + nn * NW:jj * FF + nn * NW + NW],
                        start=(jj == 0), stop=False))
                op("pe", lambda e, nn=nj, pbb=pb: e.matmul(
                    out=pbb[:, 0:NW],
                    lhsT=ones2[:, 0:P],
                    rhs=bias1m[:, nn * NW:(nn + 1) * NW],
                    start=False, stop=True), inc=True)
                ev(f"c.mm1.{i}.{nj}", "pe")
                w("act", f"c.mm1.{i}.{nj}")
                if nj == 0 and i > 0:
                    w("act", f"c.z2.{i - 1}.3")
                op("act", lambda e, nn=nj, pbb=pb: e.activation(
                    out=hbuf[:, nn * NW:(nn + 1) * NW], in_=pbb[:, 0:NW],
                    func=AF.Gelu), inc=True)
                ev(f"c.gelu.{i}.{nj}", "act")
                w("dve", f"c.gelu.{i}.{nj}")
                op("dve", lambda e, nn=nj: e.bn_stats(
                    out=st6b[:, nn * 6:(nn + 1) * 6],
                    in_=hbuf[:, nn * NW:(nn + 1) * NW]), inc=True)
                ev(f"c.hst.{i}.{nj}", "dve")

            dve_fence()
            op("dve", lambda e: e.bn_aggr(out=mv[:, 0:2], in_=st6b[:, 0:48]), inc=True)
            dve_fence()
            op("dve", lambda e: e.tensor_scalar(out=nm[:], in0=mv[:, 0:1],
                                                scalar1=-1.0, scalar2=None,
                                                op0=ALU.mult), inc=True)
            newton_rsqrt(mv[:, 1:2], EPS_LN, 1.0)

            if i > 0:
                w("pe", f"c.out.{i - 1}")
            for grp in range(4):
                if i > 0:
                    w("dve", f"c.z2T.{i - 1}.{grp}")
                op("dve", lambda e, gg=grp: e.tensor_scalar(
                    out=z2b[:, gg * 1024:(gg + 1) * 1024],
                    in0=hbuf[:, gg * 1024:(gg + 1) * 1024],
                    scalar1=nm[:, 0:1], scalar2=rs[:, 0:1],
                    op0=ALU.add, op1=ALU.mult), inc=True)
                ev(f"c.z2.{i}.{grp}", "dve")
                w("pe", f"c.z2.{i}.{grp}")
                if i > 0 and grp == 0:
                    w("act", f"c.mm2.{i - 1}")
                transpose_group(z2b[:, grp * 1024:(grp + 1) * 1024],
                                (i + grp) % 2, f"c.z2T.{i}.{grp}",
                                z2T[:, grp * 1024:(grp + 1) * 1024],
                                f"c.z2T.evac.{i}.{grp}")
                w("pe", f"c.z2T.evac.{i}.{grp}")
                for nj in range(2):
                    pb = pA2 if nj == 0 else pA3
                    for j in range(8):
                        op("pe", lambda e, jj=j, nn=nj, pbb=pb, gg=grp: e.matmul(
                            out=pbb[:, 0:NW],
                            lhsT=z2T[:, (gg * 8 + jj) * P:(gg * 8 + jj + 1) * P],
                            rhs=w2T[:, (gg * 8 + jj) * D + nn * NW:
                                    (gg * 8 + jj) * D + nn * NW + NW],
                            start=(gg == 0 and jj == 0), stop=False,
                            skip_group_check=True))
                if grp == 3:
                    for nj in range(2):
                        pb = pA2 if nj == 0 else pA3
                        op("pe", lambda e, nn=nj, pbb=pb: e.matmul(
                            out=pbb[:, 0:NW],
                            lhsT=ones2[:, 0:P],
                            rhs=bias2m[:, nn * NW:(nn + 1) * NW],
                            start=False, stop=True, skip_group_check=True),
                            inc=(nj == 1))
            ev(f"c.mm2.{i}", "pe")

            w("dve", f"c.mm2.{i}")
            if i > 0:
                w("dve", f"c.y.{i - 1}")
            for half in range(2):
                pb = pA2 if half == 0 else pA3
                s = half * NW
                op("dve", lambda e, pbb=pb, ss=s: e.tensor_tensor(
                    out=outf[:, ss:ss + NW], in0=pbb[:, 0:NW],
                    in1=xbuf[:, ss:ss + NW], op=ALU.add), inc=(half == 1))
            ev(f"c.out.{i}", "dve")
            w("sp", f"c.out.{i}")
            op("sp", lambda e: e.dma_start(out=y[i * P:(i + 1) * P, :],
                                           in_=outf[:]), inc=True, sem="sout")
            ev(f"c.y.{i}", "sout")

        for i in range(NT):
            phase_c_tile(i)

        if PHASE == "C5":
            w("pool", f"c.y.{NT - 1}")
            op("pool", lambda e: e.dma_start(out=y[0:P, :], in_=wq_s[:, 0:D]), inc=True, sem="sout")
            ev("dbg.c5a", "sout")
            w("pool", "dbg.c5a")
            op("pool", lambda e: e.dma_start(allow_slow_non_contiguous=True,
                                             out=y[0:P, 1020:1024], in_=sclb[:]),
               inc=True, sem="sout")
            ev("dbg.c5b", "sout")
            w("pool", "dbg.c5b")
            op("pool", lambda e: e.dma_start(allow_slow_non_contiguous=True,
                                             out=y[0:P, 940:1006], in_=absacc[:]),
               inc=True, sem="sout")
            op("pool", lambda e: e.dma_start(out=y[P:2 * P, :], in_=hbuf[:, 0:D]), inc=True, sem="sout")
            op("pool", lambda e: e.dma_start(out=y[2 * P:3 * P, :], in_=z2b[:, 0:D]), inc=True, sem="sout")
            op("pool", lambda e: e.dma_start(out=y[3 * P:4 * P, :], in_=outf[:]), inc=True, sem="sout")
            ev("dbg.c5", "sout")
            w("sp", "dbg.c5")

        if PHASE == "C4":
            w("pool", f"c.y.{NT - 1}")
            op("pool", lambda e: e.dma_start(
                out=y[0:P, :], in_=bias2m[:, 0:D]), inc=True, sem="sout")
            op("pool", lambda e: e.dma_start(
                out=y[P:2 * P, 0:D], in_=bias1m[:, 0:D]), inc=True, sem="sout")
            ev("dbg.c4", "sout")
            w("sp", "dbg.c4")

        if PHASE == "C3":
            w("pool", f"c.y.{NT - 1}")
            for g in range(2):
                op("pool", lambda e, gg=g: e.dma_start(
                    out=y[gg * P:(gg + 1) * P, :], in_=z2b[:, (2 + gg) * D:(3 + gg) * D]),
                    inc=True, sem="sout")
            for g in range(2):
                op("pool", lambda e, gg=g: e.dma_start(
                    out=y[(2 + gg) * P:(3 + gg) * P, :], in_=z2T[:, (2 + gg) * D:(3 + gg) * D]),
                    inc=True, sem="sout")
            ev("dbg.c3", "sout")
            w("sp", "dbg.c3")

        if PHASE == "C2":
            w("pool", f"c.y.{NT - 1}")
            for g in range(4):
                op("pool", lambda e, gg=g: e.dma_start(
                    out=y[gg * P:(gg + 1) * P, :], in_=z2b[:, gg * D:(gg + 1) * D]),
                    inc=True, sem="sout")
            ev("dbg.c2", "sout")
            w("sp", "dbg.c2")

        if PHASE == "C1":
            w("pool", f"c.y.{NT - 1}")
            op("pool", lambda e: e.dma_start(out=y[0:P, :], in_=zb[:]),
               inc=True, sem="sout")
            op("pool", lambda e: e.dma_start(out=y[P:2 * P, :], in_=hbuf[:, 0:D]),
               inc=True, sem="sout")
            op("pool", lambda e: e.dma_start(out=y[2 * P:3 * P, :], in_=z2b[:, 0:D]),
               inc=True, sem="sout")
            op("pool", lambda e: e.dma_start(out=y[3 * P:4 * P, 0:D], in_=w1T[:, 0:D]),
               inc=True, sem="sout")
            op("pool", lambda e: e.dma_start(allow_slow_non_contiguous=True,
                                             out=y[0:P, 1016:1020], in_=sclb[:]),
               inc=True, sem="sout")
            ev("dbg.c", "sout")
            w("sp", "dbg.c")

        w("sp", f"c.y.{NT - 1}")

    eng_map = {}

    def run_section(key, handle):
        eng_map[key] = handle
        E = Emitter(nc, eng_map, sems, key)
        emit_program(E)

    @block.tensor
    def _(tensor):
        run_section("pe", tensor)

    @block.scalar
    def _(scalar):
        run_section("act", scalar)

    @block.vector
    def _(vector):
        run_section("dve", vector)

    @block.gpsimd
    def _(gpsimd):
        run_section("pool", gpsimd)

    @block.sync
    def _(sync):
        run_section("sp", sync)

    stack.close()
    return nc


_NC_CACHE = {}


def kernel(L0_attn_x, L0_attn_Wq, L0_attn_Wk, L0_attn_Wv, L0_attn_Wo,
           L0_ffn_ln1_g, L0_ffn_ln1_b, L0_ffn_w1, L0_ffn_b1,
           L0_ffn_ln2_g, L0_ffn_ln2_b, L0_ffn_w2, L0_ffn_b2):
    x = np.ascontiguousarray(np.asarray(L0_attn_x, np.float32).reshape(2 * 16384, D))
    nonce = int(np.random.randint(1, 2**30))
    ident = np.eye(P, dtype=ml_dtypes.bfloat16)
    common = {
        "wq": np.ascontiguousarray(np.asarray(L0_attn_Wq, np.float32)),
        "wk": np.ascontiguousarray(np.asarray(L0_attn_Wk, np.float32)),
        "wv": np.ascontiguousarray(np.asarray(L0_attn_Wv, np.float32)),
        "wo": np.ascontiguousarray(np.asarray(L0_attn_Wo, np.float32)),
        "w1": np.ascontiguousarray(np.asarray(L0_ffn_w1, np.float32)),
        "w2": np.ascontiguousarray(np.asarray(L0_ffn_w2, np.float32)),
        "identd": ident,
        "g1cols": np.ascontiguousarray(np.asarray(L0_ffn_ln1_g, np.float32).reshape(DC, P).T),
        "b1lncols": np.ascontiguousarray(np.asarray(L0_ffn_ln1_b, np.float32).reshape(DC, P).T),
        "g2cols": np.ascontiguousarray(np.asarray(L0_ffn_ln2_g, np.float32).reshape(FC, P).T),
        "b2lncols": np.ascontiguousarray(np.asarray(L0_ffn_ln2_b, np.float32).reshape(FC, P).T),
        "b1row": np.asarray(L0_ffn_b1, np.float32).reshape(1, FF),
        "b2row": np.asarray(L0_ffn_b2, np.float32).reshape(1, D),
    }
    in_maps = []
    for c in range(8):
        mirror = c ^ 2
        m = dict(common)
        m["xo"] = np.ascontiguousarray(x[c * TPC:(c + 1) * TPC])
        m["xf"] = np.ascontiguousarray(x[mirror * TPC:(mirror + 1) * TPC])
        m["meta"] = np.array([[c % 2, nonce, 0, 0, 0, 0, 0, 0]], np.int32)
        in_maps.append(m)

    if "v1" not in _NC_CACHE:
        _NC_CACHE["v1"] = build_nc()
    nc = _NC_CACHE["v1"]
    res = run_bass_kernel_spmd(nc, in_maps, core_ids=list(range(8)))
    out = np.concatenate([res.results[c]["y"] for c in range(8)], 0)
    return out.reshape(2, 16384, D)


if __name__ == "__main__":
    nc = build_nc()
    n = sum(len(b.instructions) for f in nc.m.functions for b in f.blocks)
    print("built ok,", n, "instructions")


# version 12
# speedup vs baseline: 1.0394x; 1.0394x over previous
import numpy as np
import ml_dtypes
import concourse.bass as bass
import concourse.mybir as mybir
from concourse.bass_utils import run_bass_kernel_spmd

F32 = mybir.dt.float32
BF16 = mybir.dt.bfloat16
I32 = mybir.dt.int32
AF = mybir.ActivationFunctionType
ALU = mybir.AluOpType
AX = mybir.AxisListType

P = 128
D = 1024
H = 16
DH = 64
FF = 4096
DC = D // P
FC = FF // P
NW = 512
TPC = 4096
NT = TPC // P
EPS_LN = 1e-5
EPS_DEN = 1e-6
MAGIC = float(1.5 * 2**23)

PHASE = "ABC"

ENGINES = ("pe", "act", "dve", "pool", "sp")
DMA_SEMS = ("sxa", "swl", "sq", "sqs", "sxb", "ssp", "sxc", "stw",
            "sout", "sex", "sexp", "sexr")


class Emitter:
    def __init__(self, nc, engines, sems, active):
        self.nc = nc
        self.engines = engines
        self.sems = sems
        self.active = active
        self.counters = {name: 0 for name in list(ENGINES) + list(DMA_SEMS)}
        self.events = {}
        self.waited = {}

    def op(self, key, fn, inc=False, sem=None):
        semname = sem or key
        amt = 16 if semname in DMA_SEMS else 1
        if key == self.active:
            inst = fn(self.engines[key])
            if inc and inst is not None:
                if semname in DMA_SEMS or key == "pe":
                    inst.then_inc(self.sems[semname], amt)
                else:
                    self.engines[key].maybe_drain_then_inc(
                        (self.sems[semname], amt))
        if inc:
            self.counters[semname] += amt

    def mark(self, name, semname):
        self.events[name] = (semname, self.counters[semname])

    def wait(self, key, event):
        semname, val = self.events[event]
        if val <= 0:
            return
        prev = self.waited.get((key, semname), -1)
        if val <= prev:
            return
        self.waited[(key, semname)] = val
        if key == self.active:
            self.engines[key].wait_ge(self.sems[semname], val)

    def raw(self, key, fn):
        if key == self.active:
            return fn(self.engines[key])
        return None


def build_nc():
    nc = bass.Bass("TRN2", target_bir_lowering=False, debug=False)

    xo = nc.dram_tensor("xo", [TPC, D], F32, kind="ExternalInput").ap()
    xf = nc.dram_tensor("xf", [TPC, D], F32, kind="ExternalInput").ap()
    wqd = nc.dram_tensor("wq", [D, D], F32, kind="ExternalInput").ap()
    wkd = nc.dram_tensor("wk", [D, D], F32, kind="ExternalInput").ap()
    wvd = nc.dram_tensor("wv", [D, D], F32, kind="ExternalInput").ap()
    wod = nc.dram_tensor("wo", [D, D], F32, kind="ExternalInput").ap()
    w1d = nc.dram_tensor("w1", [FF, D], F32, kind="ExternalInput").ap()
    w2d = nc.dram_tensor("w2", [D, FF], F32, kind="ExternalInput").ap()
    identd = nc.dram_tensor("identd", [P, P], BF16, kind="ExternalInput").ap()
    g1cols = nc.dram_tensor("g1cols", [P, DC], F32, kind="ExternalInput").ap()
    b1lncols = nc.dram_tensor("b1lncols", [P, DC], F32, kind="ExternalInput").ap()
    g2cols = nc.dram_tensor("g2cols", [P, FC], F32, kind="ExternalInput").ap()
    b2lncols = nc.dram_tensor("b2lncols", [P, FC], F32, kind="ExternalInput").ap()
    b1row = nc.dram_tensor("b1row", [1, FF], F32, kind="ExternalInput").ap()
    b2row = nc.dram_tensor("b2row", [1, D], F32, kind="ExternalInput").ap()
    meta = nc.dram_tensor("meta", [1, 8], I32, kind="ExternalInput").ap()

    y = nc.dram_tensor("y", [TPC, D], F32, kind="ExternalOutput").ap()

    x1d = nc.dram_tensor("x1d", [TPC, D], F32, kind="Internal").ap()
    w1qd = nc.dram_tensor("w1qd", [FF, D], BF16, kind="Internal").ap()
    w2qd = nc.dram_tensor("w2qd", [D, FF], BF16, kind="Internal").ap()
    shdata = nc.dram_tensor("shdata", [2 * P, 520], F32, kind="Internal",
                            addr_space="Shared").ap()
    shflag = nc.dram_tensor("shflag", [1, 32], I32, kind="Internal",
                            addr_space="Shared").ap()

    SB_BASE = 16512
    SB_TOP = 229344
    ptr = [SB_BASE]

    sb_off = {}

    def sb(name, shape, dt, align=32, at=None):
        nbytes = int(np.prod(shape[1:])) * mybir.dt.size(dt)
        if at is None:
            off = (ptr[0] + align - 1) // align * align
            ptr[0] = off + nbytes
        else:
            off = at
        sb_off[name] = off
        assert max(ptr[0], off + nbytes) <= SB_TOP, f"SBUF overflow at {name}"
        return nc.alloc_sbuf_tensor_at(name, shape, dt, offset=off)

    ident = sb("ident", [P, P], BF16)
    onesrow = sb("onesrow", [1, P], BF16)
    onesrow_f = sb("onesrow_f", [1, P], F32)
    onescol_f = sb("onescol_f", [P, 1], F32)
    onescol_bf = sb("onescol_bf", [P, 1], BF16)
    mt = sb("mt", [1, 8], I32)
    flagrd = sb("flagrd", [1, 32], I32)
    g1c = sb("g1c", [P, DC], F32)
    b1lc = sb("b1lc", [P, DC], F32)
    g2c = sb("g2c", [P, FC], F32)
    b2lc = sb("b2lc", [P, FC], F32)
    biasld0 = sb("biasld0", [1, NW], F32)
    biasld1 = sb("biasld1", [1, NW], F32)
    bias_tmp = sb("bias_tmp", [1, NW], F32)
    b1lc_bf = sb("b1lc_bf", [P, DC], BF16)
    b2lc_bf = sb("b2lc_bf", [P, FC], BF16)
    absacc = sb("absacc", [P, 66], F32)
    scl = sb("scl", [1, 8], F32)
    sclb = sb("sclb", [P, 4], F32)
    st6 = sb("st6", [P, 12], F32)
    st6p = sb("st6p", [P, 12], F32)
    mvp = sb("mvp", [P, 2], F32)
    nmp = sb("nmp", [P, 1], F32)
    rsp = sb("rsp", [P, 1], F32)
    rt1p = sb("rt1p", [P, 1], F32)
    rt2p = sb("rt2p", [P, 1], F32)
    st6b = sb("st6b", [P, 48], F32)
    mv = sb("mv", [P, 2], F32)
    nm = sb("nm", [P, 1], F32)
    rs = sb("rs", [P, 1], F32)
    rt1 = sb("rt1", [P, 1], F32)
    rt2 = sb("rt2", [P, 1], F32)
    qred = sb("qred", [P, 1], F32)
    kv_mine = sb("kv_mine", [P, 520], F32)
    kv_part = sb("kv_part", [P, 520], F32)
    kv_ext = sb("kv_ext", [P, 520], BF16)
    recip = sb("recip", [P, H], F32)
    den_s = sb("den_s", [P, H], F32)
    ones2 = sb("ones2", [P, P], BF16)
    bias1m = sb("bias1m", [P, FF], BF16)
    bias2m = sb("bias2m", [P, D], BF16)

    DYN = (ptr[0] + 255) // 256 * 256

    ptr[0] = DYN
    wq_s = sb("wq_s", [P, DC * D], BF16)
    wk_s = sb("wk_s", [P, DC * D], BF16)
    wv_s = sb("wv_s", [P, DC * D], BF16)
    wo_s = sb("wo_s", [P, DC * D], BF16)
    stash = sb("stash", [P, NT * D], BF16)
    xb0 = sb("xb0", [P, D], F32)
    xb1 = sb("xb1", [P, D], F32)
    lnb0 = sb("lnb0", [P, D], BF16)
    lnb1 = sb("lnb1", [P, D], BF16)
    lnT0 = sb("lnT0", [P, D], BF16)
    lnT1 = sb("lnT1", [P, D], BF16)
    kf0 = sb("kf0", [P, D], BF16)
    kf1 = sb("kf1", [P, D], BF16)
    v0 = sb("v0", [P, D], BF16)
    v1 = sb("v1", [P, D], BF16)
    qf_b = sb("qf_b", [P, D], BF16)
    tmpf = sb("tmpf", [P, D], F32)
    tmpe = sb("tmpe", [P, D], F32)
    wt0 = sb("wt0", [P, D], F32)
    wt1 = sb("wt1", [P, D], F32)
    wqt0 = sb("wqt0", [P, D], BF16)
    wqt1 = sb("wqt1", [P, D], BF16)
    outb = sb("outb", [P, D], BF16)
    outT = sb("outT", [P, D], BF16)
    x1f = sb("x1f", [P, D], F32)
    wst0 = sb("wst0", [P, D], F32, at=sb_off["wt0"])
    wst1 = sb("wst1", [P, D], F32, at=sb_off["wt1"])
    wst2 = sb("wst2", [P, D], F32, at=sb_off["outb"])
    wst3 = sb("wst3", [P, D], F32, at=sb_off["x1f"])
    AB_END = ptr[0]

    ptr[0] = DYN
    w1T = sb("w1T", [P, DC * FF], BF16)
    w2T = sb("w2T", [P, FC * D], BF16)
    cx0 = sb("cx0", [P, D], F32)
    cx1 = sb("cx1", [P, D], F32)
    zb = sb("zb", [P, D], BF16)
    zT = sb("zT", [P, D], BF16, at=sb_off["zb"])
    hbuf = sb("hbuf", [P, FF], F32)
    z2b = sb("z2b", [P, FF], BF16)
    z2T = sb("z2T", [P, FF], BF16)
    outf = sb("outf", [P, D], F32)
    C_END = ptr[0]
    absrep = sb("absrep", [P, 256], F32, at=sb_off["tmpf"])
    bcr2 = sb("bcr2", [P, FF], BF16, at=sb_off["hbuf"])
    bcr1 = sb("bcr1", [P, D], BF16, at=sb_off["hbuf"] + 8192)

    pT0 = nc.alloc_psum_tensor("pT0", [P, 1024], BF16)
    pT1 = nc.alloc_psum_tensor("pT1", [P, 1024], BF16)
    pA0 = nc.alloc_psum_tensor("pA0", [P, NW], F32)
    pA1 = nc.alloc_psum_tensor("pA1", [P, NW], F32)
    pA2 = nc.alloc_psum_tensor("pA2", [P, NW], F32)
    pA3 = nc.alloc_psum_tensor("pA3", [P, NW], F32)
    pKv = nc.alloc_psum_tensor("pKv", [P, NW], F32)
    pK1 = nc.alloc_psum_tensor("pK1", [P, H], F32)

    from contextlib import ExitStack
    stack = ExitStack()
    sems = {}
    for name in list(ENGINES) + list(DMA_SEMS):
        sems[name] = stack.enter_context(nc.semaphore(name))
    block = stack.enter_context(nc.Block())

    def a_prev(tag, i, k=1):
        g = (i if tag == "ao" else NT + i) - k
        if g < 0:
            return None
        return ("ao", g) if g < NT else ("af", g - NT)

    def emit_program(E):
        ev, w, op = E.mark, E.wait, E.op
        fence_n = [0]

        def dve_fence():
            name = f"fence.{fence_n[0]}"
            fence_n[0] += 1
            ev(name, "dve")
            w("dve", name)

        def pe_drain():
            op("pe", lambda e: e.drain())

        op("sp", lambda e: e.dma_start(out=ident[:], in_=identd[:, :]), inc=True, sem="swl")
        op("sp", lambda e: e.dma_start(out=mt[:], in_=meta[:, :]), inc=True, sem="swl")
        for src, dst in [(g1cols, g1c), (b1lncols, b1lc), (g2cols, g2c),
                         (b2lncols, b2lc)]:
            op("sp", lambda e, s=src, d=dst: e.dma_start(out=d[:], in_=s[:, :]),
               inc=True, sem="swl")
        ev("init.loads", "swl")

        wcnt = 0
        for wsrc, wdst, tag in [(wqd, wq_s, "wq"), (wkd, wk_s, "wk"),
                                (wvd, wv_s, "wv"), (wod, wo_s, "wo")]:
            for j in range(DC):
                cnt = wcnt
                wcnt += 1
                wst = (wst0, wst1, wst2, wst3)[cnt % 4]
                if cnt >= 4:
                    w("sp", f"wl.cast.{cnt - 4}")
                op("sp", lambda e, s=wsrc, ws=wst, jj=j:
                   e.dma_start(out=ws[:], in_=s[jj * P:(jj + 1) * P, :]),
                   inc=True, sem="swl")
                ev(f"wl.ld.{cnt}", "swl")
                w("dve", f"wl.ld.{cnt}")
                op("dve", lambda e, d=wdst, ws=wst, jj=j:
                   e.tensor_copy(out=d[:, jj * D:(jj + 1) * D], in_=ws[:]),
                   inc=True)
                ev(f"wl.cast.{cnt}", "dve")
            ev(f"wl.{tag}", "dve")

        w("dve", "init.loads")
        op("dve", lambda e: e.memset(onesrow[:], 1.0), inc=True)
        op("dve", lambda e: e.memset(onesrow_f[:], 1.0), inc=True)
        op("dve", lambda e: e.memset(onescol_f[:], 1.0), inc=True)
        op("dve", lambda e: e.memset(onescol_bf[:], 1.0), inc=True)
        op("dve", lambda e: e.memset(absacc[:], 0.0), inc=True)
        op("dve", lambda e: e.tensor_copy(out=b1lc_bf[:], in_=b1lc[:]), inc=True)
        op("dve", lambda e: e.tensor_copy(out=b2lc_bf[:], in_=b2lc[:]), inc=True)
        op("dve", lambda e: e.memset(ones2[:], 0.0), inc=True)
        op("dve", lambda e: e.memset(ones2[0:1, :], 1.0), inc=True)
        op("dve", lambda e: e.memset(ones2[32:33, :], 1.0), inc=True)
        op("dve", lambda e: e.memset(bias1m[:], 0.0), inc=True)
        op("dve", lambda e: e.memset(bias2m[:], 0.0), inc=True)
        ev("init.dve", "dve")

        scr0 = dict(st6=st6, mv=mv, nm=nm, rs=rs, rt1=rt1, rt2=rt2)
        scr1 = dict(st6=st6p, mv=mvp, nm=nmp, rs=rsp, rt1=rt1p, rt2=rt2p)

        def newton_rsqrt(var_ap, eps_add, var_scale, S=scr0):
            nme = f"rsq.{fence_n[0]}"
            fence_n[0] += 1
            op("dve", lambda e: e.tensor_scalar(out=S["rt1"][:], in0=var_ap,
                                                scalar1=float(var_scale),
                                                scalar2=float(eps_add),
                                                op0=ALU.mult, op1=ALU.add), inc=True)
            ev(f"{nme}.a", "dve")
            w("act", f"{nme}.a")
            op("act", lambda e: e.activation(out=S["rt2"][:], in_=S["rt1"][:],
                                             func=AF.Ln), inc=True)
            ev(f"{nme}.ln", "act")
            w("act", f"{nme}.ln")
            op("act", lambda e: e.activation(out=S["rs"][:], in_=S["rt2"][:],
                                             func=AF.Exp, scale=-0.5), inc=True)
            ev(f"{nme}.b", "act")
            w("dve", f"{nme}.b")
            return f"{nme}.b"

        def ln_stats(xsb, eps_add, var_scale, S=scr0):
            op("dve", lambda e: e.bn_stats(out=S["st6"][:, 0:6], in_=xsb[:, 0:512]), inc=True)
            op("dve", lambda e: e.bn_stats(out=S["st6"][:, 6:12], in_=xsb[:, 512:1024]), inc=True)
            dve_fence()
            op("dve", lambda e: e.bn_aggr(out=S["mv"][:, 0:2], in_=S["st6"][:, 0:12]), inc=True)
            dve_fence()
            op("dve", lambda e: e.tensor_scalar(out=S["nm"][:], in0=S["mv"][:, 0:1],
                                                scalar1=-1.0, scalar2=None,
                                                op0=ALU.mult), inc=True)
            return newton_rsqrt(S["mv"][:, 1:2], eps_add, var_scale, S)
        q_tiles = [("w1", i) for i in range(FC)] + [("w2", i, j) for i in range(DC)
                                                   for j in range(4)]

        def q_src_ap(t):
            if t[0] == "w1":
                return w1d[t[1] * P:(t[1] + 1) * P, :]
            _, i, j = t
            return w2d[i * P:(i + 1) * P, j * D:(j + 1) * D]

        def q_dst_ap(t):
            if t[0] == "w1":
                return w1qd[t[1] * P:(t[1] + 1) * P, :]
            _, i, j = t
            return w2qd[i * P:(i + 1) * P, j * D:(j + 1) * D]

        q1_idx = [0]

        def q1_step():
            idx = q1_idx[0]
            if idx >= len(q_tiles):
                return
            q1_idx[0] += 1
            t = q_tiles[idx]
            buf = wt0 if idx % 2 == 0 else wt1
            col = 0 if t[0] == "w1" else 1
            if idx >= 2:
                w("sp", f"q1.red.{idx - 2}")
            else:
                w("sp", f"wl.cast.{28 + idx}")
            op("sp", lambda e: e.dma_start(out=buf[:], in_=q_src_ap(t)),
               inc=True, sem="sq")
            ev(f"q1.load.{idx}", "sq")
            w("dve", f"q1.load.{idx}")
            op("dve", lambda e, bb=buf: e.tensor_scalar(
                out=bb[:].bitcast(I32), in0=bb[:].bitcast(I32),
                scalar1=0x7FFFFFFF, scalar2=None,
                op0=ALU.bitwise_and), inc=True)
            op("dve", lambda e, ii=idx, bb=buf: e.tensor_scalar(
                out=bb[:], in0=bb[:], scalar1=0.0, scalar2=None,
                op0=ALU.add, op1=ALU.add,
                accum_out=absacc[:, 2 + ii:3 + ii]), inc=True)
            ev(f"q1.red.{idx}", "dve")

        state = {"prev_exp": None}
        bank_last = {0: None, 1: None}
        last_rsq = {0: None, 1: None}

        def elu_evac(tag_ev, pb0, pb1, dstbf):
            if state["prev_exp"] is not None:
                w("dve", state["prev_exp"])
            for half in range(2):
                pb = pb0 if half == 0 else pb1
                s = half * NW
                op("dve", lambda e, pbb=pb, ss=s: e.tensor_scalar(
                    out=tmpf[:, ss:ss + NW], in0=pbb[:, 0:NW],
                    scalar1=0.0, scalar2=None, op0=ALU.min), inc=(half == 1))
            ev(f"{tag_ev}.min", "dve")
            w("act", f"{tag_ev}.min")
            op("act", lambda e: e.activation(out=tmpe[:], in_=tmpf[:],
                                             func=AF.Exp), inc=True)
            ev(f"{tag_ev}.exp", "act")
            state["prev_exp"] = f"{tag_ev}.exp"
            w("dve", f"{tag_ev}.exp")
            for half in range(2):
                pb = pb0 if half == 0 else pb1
                s = half * NW
                op("dve", lambda e, pbb=pb, ss=s: e.scalar_tensor_tensor(
                    out=dstbf[:, ss:ss + NW], in0=pbb[:, 0:NW], scalar=0.0,
                    in1=tmpe[:, ss:ss + NW], op0=ALU.max, op1=ALU.add),
                    inc=(half == 1))
            ev(tag_ev, "dve")

        def transpose_group(srcbf, bank_id, evname, dst_sb, evac_evname):
            pb = pT0 if bank_id == 0 else pT1
            if bank_last[bank_id] is not None:
                w("pe", bank_last[bank_id])
            pe_drain()
            for j in range(DC):
                op("pe", lambda e, jj=j: e.transpose(
                    out=pb[:, jj * P:(jj + 1) * P],
                    in_=srcbf[:, jj * P:(jj + 1) * P],
                    identity=ident[:]), inc=(j == DC - 1))
            pe_drain()
            ev(evname, "pe")
            w("act", evname)
            op("act", lambda e: e.copy(out=dst_sb, in_=pb[:]), inc=True)
            ev(evac_evname, "act")
            bank_last[bank_id] = evac_evname

        def mm_proj(lnTb, wsb, pb0, pb1, evname, wl_ev):
            w("pe", wl_ev)
            for j in range(DC):
                for half in range(2):
                    pb = pb0 if half == 0 else pb1
                    op("pe", lambda e, jj=j, hh=half, pbb=pb: e.matmul(
                        out=pbb[:, 0:NW],
                        lhsT=lnTb[:, jj * P:(jj + 1) * P],
                        rhs=wsb[:, jj * D + hh * NW: jj * D + hh * NW + NW],
                        start=(jj == 0), stop=(jj == DC - 1)),
                        inc=(j == DC - 1 and half == 1))
            ev(evname, "pe")

        def a_stage1(tag, i):
            foreign = tag == "af"
            g = i if tag == "ao" else NT + i
            p = g % 2
            xbuf = xb0 if p == 0 else xb1
            lnb = lnb0 if p == 0 else lnb1
            lnT = lnT0 if p == 0 else lnT1
            S = scr0 if p == 0 else scr1
            xsrc = xf if foreign else xo
            p2 = a_prev(tag, i, 2)

            if p2 is not None:
                w("sp", f"{p2[0]}.lnb.{p2[1]}")
            op("sp", lambda e: e.dma_start(out=xbuf[:],
                                           in_=xsrc[i * P:(i + 1) * P, :]),
               inc=True, sem="sxa")
            ev(f"{tag}.x.{i}", "sxa")

            w("dve", f"{tag}.x.{i}")
            if p2 is not None:
                w("dve", f"{p2[0]}.T.{p2[1]}")
            if last_rsq[p] is not None:
                w("dve", last_rsq[p])
            rev = ln_stats(xbuf, EPS_LN, 1.0, S)
            last_rsq[p] = rev
            op("dve", lambda e, lb=lnb, xb=xbuf, SS=S: e.tensor_scalar(
                out=lb[:], in0=xb[:],
                scalar1=SS["nm"][:, 0:1], scalar2=SS["rs"][:, 0:1],
                op0=ALU.add, op1=ALU.mult), inc=True)
            ev(f"{tag}.lnb.{i}", "dve")

            w("pe", f"{tag}.lnb.{i}")
            if p2 is not None:
                w("act", f"{p2[0]}.V.{p2[1]}")
            transpose_group(lnb, p, f"{tag}.T.{i}", lnT[:], f"{tag}.lnT.evac.{i}")

        def a_stage2(tag, i):
            foreign = tag == "af"
            g = i if tag == "ao" else NT + i
            p = g % 2
            lnT = lnT0 if p == 0 else lnT1
            kf_b = kf0 if p == 0 else kf1
            v_b = v0 if p == 0 else v1
            p1 = a_prev(tag, i, 1)
            p2 = a_prev(tag, i, 2)

            w("pe", f"{tag}.lnT.evac.{i}")

            if not foreign:
                if p1 is not None:
                    w("pe", f"{p1[0]}.v.evac.{p1[1]}")
                mm_proj(lnT, wq_s, pA0, pA1, f"{tag}.Q.{i}", "wl.wq")
                w("dve", f"{tag}.Q.{i}")
                if i > 0:
                    w("dve", f"ao.qfT.{i - 1}")
                elu_evac(f"ao.qf.{i}", pA0, pA1, qf_b)
                w("pe", f"ao.qf.{i}")
                transpose_group(qf_b, 1 - p, f"ao.qfT.{i}",
                                stash[:, i * D:(i + 1) * D], f"ao.qfT.evac.{i}")

            if p1 is not None:
                w("pe", f"{p1[0]}.kf.{p1[1]}")
            mm_proj(lnT, wk_s, pA2, pA3, f"{tag}.K.{i}", "wl.wk")
            w("dve", f"{tag}.K.{i}")
            if p2 is not None:
                w("dve", f"{p2[0]}.kv.{p2[1]}")
            elu_evac(f"{tag}.kf.{i}", pA2, pA3, kf_b)

            if not foreign:
                w("pe", f"ao.qf.{i}")
            elif p1 is not None:
                w("pe", f"{p1[0]}.v.evac.{p1[1]}")
            mm_proj(lnT, wv_s, pA0, pA1, f"{tag}.V.{i}", "wl.wv")
            w("dve", f"{tag}.V.{i}")
            if p2 is not None:
                w("dve", f"{p2[0]}.kv.{p2[1]}")
            for half in range(2):
                pb = pA0 if half == 0 else pA1
                s = half * NW
                op("dve", lambda e, pbb=pb, ss=s, vb=v_b: e.tensor_copy(
                    out=vb[:, ss:ss + NW], in_=pbb[:, 0:NW]), inc=(half == 1))
            ev(f"{tag}.v.evac.{i}", "dve")

            w("pe", f"{tag}.kf.{i}")
            w("pe", f"{tag}.v.evac.{i}")
            first = (tag == "ao" and i == 0)
            for h in range(H):
                pb = 64 * (h % 2)
                cb = 64 * (h // 2)
                st = first and h < 2
                op("pe", lambda e, hh=h, pbb=pb, cbb=cb, stt=st, kb=kf_b, vb=v_b: e.matmul(
                    out=pKv[pbb:pbb + 64, cbb:cbb + 64],
                    lhsT=kb[:, hh * DH:(hh + 1) * DH],
                    rhs=vb[:, hh * DH:(hh + 1) * DH],
                    start=stt, stop=False, skip_group_check=True))
                op("pe", lambda e, hh=h, pbb=pb, stt=st, kb=kf_b: e.matmul(
                    out=pK1[pbb:pbb + 64, hh // 2:hh // 2 + 1],
                    lhsT=kb[:, hh * DH:(hh + 1) * DH],
                    rhs=onescol_bf[:, 0:1],
                    start=stt, stop=False, skip_group_check=True),
                    inc=(h == H - 1))
            ev(f"{tag}.kv.{i}", "pe")

        w("dve", "init.dve")
        glist = [("ao", i) for i in range(NT)] + [("af", i) for i in range(NT)]
        a_stage1(*glist[0])
        for g in range(len(glist)):
            if g + 1 < len(glist):
                a_stage1(*glist[g + 1])
            a_stage2(*glist[g])
            q1_step()
        while q1_idx[0] < len(q_tiles):
            q1_step()

        w("dve", f"af.kv.{NT - 1}")
        op("dve", lambda e: e.tensor_copy(out=kv_mine[:, 0:512], in_=pKv[:, 0:NW]), inc=True)
        op("dve", lambda e: e.tensor_copy(out=kv_mine[:, 512:520], in_=pK1[:, 0:8]), inc=True)
        ev("kv.mine", "dve")

        w("dve", f"q1.red.{len(q_tiles) - 1}")
        dve_fence()
        op("dve", lambda e: e.tensor_reduce(out=absacc[:, 0:1],
                                            in_=absacc[:, 2:2 + FC],
                                            axis=AX.X, op=ALU.add), inc=True)
        op("dve", lambda e: e.tensor_reduce(out=absacc[:, 1:2],
                                            in_=absacc[:, 2 + FC:2 + 2 * FC],
                                            axis=AX.X, op=ALU.add), inc=True)
        dve_fence()
        for col in range(2):
            op("dve", lambda e, cc=col: e.tensor_scalar(
                out=absrep[:, cc * P:(cc + 1) * P], in0=xb0[:, 0:P],
                scalar1=0.0, scalar2=absacc[:, cc:cc + 1],
                op0=ALU.mult, op1=ALU.add), inc=(col == 1))
        ev("q.absrep", "dve")
        w("pe", "q.absrep")
        w("pe", f"af.v.evac.{NT - 1}")
        w("pe", f"af.kf.{NT - 1}")
        for col in range(2):
            op("pe", lambda e, cc=col: e.matmul(
                out=pA0[:, cc:cc + 1],
                lhsT=absrep[:, cc * P:(cc + 1) * P],
                rhs=onescol_f[:, 0:1],
                start=(cc == 0), stop=(cc == 1), skip_group_check=True),
                inc=(col == 1))
        ev("q.sums", "pe")
        w("dve", "q.sums")
        op("dve", lambda e: e.tensor_scalar(out=sclb[:, 2:3], in0=pA0[:, 0:1],
                                            scalar1=float(1.0 / (FF * D)),
                                            scalar2=1e-5, op0=ALU.mult,
                                            op1=ALU.max), inc=True)
        op("dve", lambda e: e.tensor_scalar(out=sclb[:, 3:4], in0=pA0[:, 1:2],
                                            scalar1=float(1.0 / (D * FF)),
                                            scalar2=1e-5, op0=ALU.mult,
                                            op1=ALU.max), inc=True)
        op("dve", lambda e: e.reciprocal(out=sclb[:, 0:1], in_=sclb[:, 2:3]), inc=True)
        op("dve", lambda e: e.reciprocal(out=sclb[:, 1:2], in_=sclb[:, 3:4]), inc=True)
        ev("q.sclb", "dve")

        def exchange(g):
            slot = g.value_load(mt[0:1, 0:1])
            g.dma_start(out=shdata[bass.ts(slot, P), 0:520],
                        in_=kv_mine[:]).then_inc(sems["sex"], 16)
            g.wait_ge(sems["sex"], 16)
            g.dma_start(out=shflag[0:1, bass.ts(slot, 16)][0:1, 0:1],
                        in_=mt[0:1, 1:2]).then_inc(sems["sex"], 16)
            g.wait_ge(sems["sex"], 32)
            nonce = g.value_load(mt[0:1, 1:2])
            with g.register("wp") as wp, g.register("cr") as cr:
                g.reg_mov(wp, 0)
                g.reg_mov(cr, 1)
                with g.While(cr):
                    g.dma_start(out=flagrd[0:1, 0:32],
                                in_=shflag[0:1, 0:32]).then_inc(sems["sexp"], 16)
                    g.reg_alu(wp, wp, 16, ALU.add)
                    g.wait_ge(sems["sexp"], wp)
                    f0 = g.value_load(flagrd[0:1, 0:1])
                    f1 = g.value_load(flagrd[0:1, 16:17])
                    e0 = g.compute_val(f0 == nonce)
                    e1 = g.compute_val(f1 == nonce)
                    sm = g.compute_val(e0 + e1)
                    dn = g.compute_val(sm != 2)
                    g.reg_mov(cr, dn)
                    g.nop(cycle_cnt=250)
            prt = g.compute_val(slot == 0)
            g.dma_start(out=kv_part[:], in_=shdata[bass.ts(prt, P), 0:520]
                        ).then_inc(sems["sexr"], 16)
            g.wait_ge(sems["sexr"], 16)

        w("pool", "kv.mine")
        w("pool", "init.loads")
        E.raw("pool", exchange)
        op("pool", lambda e: e.nop(), inc=True)
        ev("kv.exchanged", "pool")

        w("dve", "kv.exchanged")
        op("dve", lambda e: e.tensor_tensor(out=kv_mine[:], in0=kv_mine[:],
                                            in1=kv_part[:], op=ALU.add), inc=True)
        for h in range(H):
            pb = 64 * (h % 2)
            op("dve", lambda e, hh=h, pbb=pb: e.tensor_copy(
                out=kv_ext[pbb:pbb + 64, 65 * (hh // 2):65 * (hh // 2) + 64],
                in_=kv_mine[pbb:pbb + 64, 64 * (hh // 2):64 * (hh // 2) + 64]),
                inc=(h == H - 1))
        for h in range(H):
            pb = 64 * (h % 2)
            op("dve", lambda e, hh=h, pbb=pb: e.tensor_copy(
                out=kv_ext[pbb:pbb + 64, 65 * (hh // 2) + 64:65 * (hh // 2) + 65],
                in_=kv_mine[pbb:pbb + 64, 512 + hh // 2:512 + hh // 2 + 1]),
                inc=(h == H - 1))
        ev("kv.ext", "dve")

        if PHASE == "A":
            w("sp", "kv.ext")
            op("sp", lambda e: e.dma_start(out=y[0:P, 0:520], in_=kv_mine[:]),
               inc=True, sem="sout")
            w("pool", "kv.ext")
            op("pool", lambda e: e.dma_start(out=y[P:2 * P, :], in_=lnb[:]),
               inc=True, sem="sout")
            op("pool", lambda e: e.dma_start(out=y[2 * P:3 * P, :], in_=kf_b[:]),
               inc=True, sem="sout")
            op("pool", lambda e: e.dma_start(out=y[3 * P:4 * P, :], in_=v_b[:]),
               inc=True, sem="sout")
            op("pool", lambda e: e.dma_start(allow_slow_non_contiguous=True, out=y[0:P, 520:522], in_=mv[:]),
               inc=True, sem="sout")
            op("pool", lambda e: e.dma_start(allow_slow_non_contiguous=True, out=y[0:P, 522:523], in_=nm[:]),
               inc=True, sem="sout")
            op("pool", lambda e: e.dma_start(allow_slow_non_contiguous=True, out=y[0:P, 523:524], in_=rs[:]),
               inc=True, sem="sout")
            op("pool", lambda e: e.dma_start(allow_slow_non_contiguous=True, out=y[0:P, 524:525], in_=rt1[:]),
               inc=True, sem="sout")
            ev("dbg.y", "sout")
            w("sp", "dbg.y")
            return

        NQ = len(q_tiles)
        w("pool", "q.sclb")
        for idx, t in enumerate(q_tiles):
            buf = wt0 if idx % 2 == 0 else wt1
            qbuf = wqt0 if idx % 2 == 0 else wqt1
            scol = 0 if t[0] == "w1" else 1
            if idx >= 2:
                w("sp", f"q2.quant.{idx - 2}")
                w("sp", f"q2.store.{idx - 2}")
            else:
                w("sp", f"q1.red.{NQ - 2 + idx}")
            op("sp", lambda e, tt=t, bb=buf: e.dma_start(out=bb[:], in_=q_src_ap(tt)),
               inc=True, sem="sq")
            ev(f"q2.load.{idx}", "sq")
            w("pool", f"q2.load.{idx}")
            if idx >= 2:
                w("pool", f"q2.store.{idx - 2}")
            op("pool", lambda e, bb=buf, sc=scol: e.tensor_scalar(
                out=bb[:], in0=bb[:], scalar1=sclb[:, sc:sc + 1],
                scalar2=MAGIC, op0=ALU.mult, op1=ALU.add), inc=True)
            op("pool", lambda e, bb=buf: e.tensor_scalar(
                out=bb[:], in0=bb[:], scalar1=-MAGIC, scalar2=1.0,
                op0=ALU.add, op1=ALU.mult), inc=True)
            op("pool", lambda e, bb=buf, qb=qbuf: e.tensor_scalar(
                out=qb[:], in0=bb[:], scalar1=1.0, scalar2=-1.0,
                op0=ALU.min, op1=ALU.max), inc=True)
            ev(f"q2.quant.{idx}", "pool")
            op("pool", lambda e, tt=t, qb=qbuf: e.dma_start(out=q_dst_ap(tt), in_=qb[:]),
               inc=True, sem="sqs")
            ev(f"q2.store.{idx}", "sqs")

        def phase_b_tile(i):
            xbuf = xb0 if i % 2 == 0 else xb1
            if i >= 2:
                w("sp", f"b.res.{i - 2}")
            else:
                w("sp", f"af.lnb.{NT - 2 + i}")
            op("sp", lambda e: e.dma_start(out=xbuf[:],
                                           in_=xo[i * P:(i + 1) * P, :]),
               inc=True, sem="sxb")
            ev(f"b.x.{i}", "sxb")

            if i == 0:
                w("pe", "kv.ext")
                w("pe", "q.sclb")
            else:
                w("pe", f"b.div.dve.{i - 1}")
                w("pe", f"b.div.act.{i - 1}")
            for h in range(H):
                pb = pA0 if h % 2 == 0 else pA1
                pd = pK1 if h % 2 == 0 else pKv
                cb = 64 * (h // 2)
                op("pe", lambda e, hh=h, pbb=pb, cbb=cb: e.matmul(
                    out=pbb[:, cbb:cbb + 64],
                    lhsT=stash[64 * (hh % 2):64 * (hh % 2) + 64,
                               i * D + (hh // 2) * P:i * D + (hh // 2) * P + P],
                    rhs=kv_ext[64 * (hh % 2):64 * (hh % 2) + 64,
                               65 * (hh // 2):65 * (hh // 2) + 64],
                    start=(hh < 2), stop=False, skip_group_check=True))
                op("pe", lambda e, hh=h, pdd=pd: e.matmul(
                    out=pdd[:, hh // 2:hh // 2 + 1],
                    lhsT=stash[64 * (hh % 2):64 * (hh % 2) + 64,
                               i * D + (hh // 2) * P:i * D + (hh // 2) * P + P],
                    rhs=kv_ext[64 * (hh % 2):64 * (hh % 2) + 64,
                               65 * (hh // 2) + 64:65 * (hh // 2) + 65],
                    start=(hh < 2), stop=True, skip_group_check=True),
                    inc=(h == H - 1))
            ev(f"b.num.{i}", "pe")

            w("dve", f"b.num.{i}")
            if i > 0:
                w("dve", f"b.div.act.{i - 1}")
            op("dve", lambda e: e.tensor_scalar(out=den_s[:, 0:8], in0=pK1[:, 0:8],
                                                scalar1=EPS_DEN, scalar2=None,
                                                op0=ALU.add), inc=True)
            op("dve", lambda e: e.tensor_scalar(out=den_s[:, 8:16], in0=pKv[:, 0:8],
                                                scalar1=EPS_DEN, scalar2=None,
                                                op0=ALU.add), inc=True)
            op("dve", lambda e: e.reciprocal(out=recip[:], in_=den_s[:]), inc=True)
            ev(f"b.recip.{i}", "dve")

            w("dve", f"b.recip.{i}")
            w("act", f"b.recip.{i}")
            if i > 0:
                w("dve", f"b.outT.{i - 1}")
                w("act", f"b.outT.{i - 1}")
            for h in range(H):
                pb = pA0 if h % 2 == 0 else pA1
                cb = 64 * (h // 2)
                rc = (h // 2) + 8 * (h % 2)
                if h % 2 == 0:
                    op("dve", lambda e, hh=h, pbb=pb, cbb=cb, rcc=rc: e.tensor_scalar(
                        out=outb[:, hh * DH:(hh + 1) * DH],
                        in0=pbb[:, cbb:cbb + 64],
                        scalar1=recip[:, rcc:rcc + 1], scalar2=None,
                        op0=ALU.mult), inc=(h == H - 2))
                else:
                    op("act", lambda e, hh=h, pbb=pb, cbb=cb, rcc=rc: e.activation(
                        out=outb[:, hh * DH:(hh + 1) * DH],
                        in_=pbb[:, cbb:cbb + 64], func=AF.Copy,
                        scale=recip[:, rcc:rcc + 1]), inc=(h == H - 1))
            ev(f"b.div.dve.{i}", "dve")
            ev(f"b.div.act.{i}", "act")

            w("pe", f"b.div.dve.{i}")
            w("pe", f"b.div.act.{i}")
            if i > 0:
                w("act", f"b.wo.{i - 1}")
            transpose_group(outb, i % 2, f"b.outT.{i}", outT[:],
                            f"b.outT.evac.{i}")

            w("pe", f"b.outT.evac.{i}")
            w("pe", "wl.wo")
            if i > 0:
                w("pe", f"b.res.{i - 1}")
            for j in range(DC):
                for half in range(2):
                    pb = pA2 if half == 0 else pA3
                    op("pe", lambda e, jj=j, hh=half, pbb=pb: e.matmul(
                        out=pbb[:, 0:NW],
                        lhsT=outT[:, jj * P:(jj + 1) * P],
                        rhs=wo_s[:, jj * D + hh * NW: jj * D + hh * NW + NW],
                        start=(jj == 0), stop=(jj == DC - 1)),
                        inc=(j == DC - 1 and half == 1))
            ev(f"b.wo.{i}", "pe")

            w("dve", f"b.wo.{i}")
            w("dve", f"b.x.{i}")
            if i > 0:
                w("dve", f"b.x1.{i - 1}")
            for half in range(2):
                pb = pA2 if half == 0 else pA3
                s = half * NW
                op("dve", lambda e, pbb=pb, ss=s: e.tensor_tensor(
                    out=x1f[:, ss:ss + NW], in0=pbb[:, 0:NW],
                    in1=xbuf[:, ss:ss + NW], op=ALU.add), inc=(half == 1))
            ev(f"b.res.{i}", "dve")
            w("sp", f"b.res.{i}")
            op("sp", lambda e: e.dma_start(out=x1d[i * P:(i + 1) * P, :],
                                           in_=x1f[:]), inc=True, sem="ssp")
            ev(f"b.x1.{i}", "ssp")

        for i in range(NT):
            phase_b_tile(i)

        if PHASE == "B":
            for i in range(NT):
                w("sp", f"b.x1.{i}")
                op("sp", lambda e, ii=i: e.dma_start(out=xb0[:], in_=x1d[ii * P:(ii + 1) * P, :]),
                   inc=True, sem="sxc")
                ev(f"dbg.l.{i}", "sxc")
                w("sp", f"dbg.l.{i}")
                op("sp", lambda e, ii=i: e.dma_start(out=y[ii * P:(ii + 1) * P, :], in_=xb0[:]),
                   inc=True, sem="sout")
                ev(f"dbg.s.{i}", "sout")
                w("sp", f"dbg.s.{i}")
            return

        w("act", f"b.wo.{NT - 1}")
        w("act", f"b.res.{NT - 1}")
        w("act", f"b.x1.{NT - 1}")
        w("act", "kv.exchanged")
        w("act", "q2.store.31")
        for j in range(DC):
            op("act", lambda e, jj=j: e.dma_start(
                out=w1T[:, jj * FF:(jj + 1) * FF],
                in_=w1qd[:, jj * P:(jj + 1) * P], transpose=True),
                inc=True, sem="stw")
        w("act", f"q2.store.{NQ - 1}")
        for j in range(FC):
            op("act", lambda e, jj=j: e.dma_start(
                out=w2T[:, jj * D:(jj + 1) * D],
                in_=w2qd[:, jj * P:(jj + 1) * P], transpose=True),
                inc=True, sem="stw")
        ev("c.wT", "stw")

        w("pe", "c.wT")
        w("pe", f"b.wo.{NT - 1}")
        w("dve", "kv.ext")
        for jj in range(DC):
            op("dve", lambda e, j_=jj: e.tensor_scalar(
                out=bcr1[:, j_ * P:(j_ + 1) * P], in0=ident[:, 0:P],
                scalar1=0.0, scalar2=b1lc[:, j_:j_ + 1],
                op0=ALU.mult, op1=ALU.add), inc=(jj == DC - 1))
        for jj in range(FC):
            op("dve", lambda e, j_=jj: e.tensor_scalar(
                out=bcr2[:, j_ * P:(j_ + 1) * P], in0=ident[:, 0:P],
                scalar1=0.0, scalar2=b2lc[:, j_:j_ + 1],
                op0=ALU.mult, op1=ALU.add), inc=(jj == FC - 1))
        ev("c.bcr", "dve")
        bias_jobs = [("b1", nj) for nj in range(8)] + [("b2", nj) for nj in range(2)]
        for bj, (kind, nj) in enumerate(bias_jobs):
            bld = biasld0 if bj % 2 == 0 else biasld1
            brow_src = b1row if kind == "b1" else b2row
            if bj >= 1:
                w("sp", f"c.bf.{bj - 1}")
            w("sp", f"b.x1.{NT - 1}")
            op("sp", lambda e, s=brow_src, bb=bld, nn=nj: e.dma_start(
                out=bb[0:1, 0:NW], in_=s[0:1, nn * NW:(nn + 1) * NW]),
                inc=True, sem="swl")
            ev(f"c.bld.{bj}", "swl")
            if bj > 0:
                w("pe", f"c.bf.{bj - 1}")
            else:
                w("pe", "c.bcr")
            if kind == "b1":
                for j in range(DC):
                    op("pe", lambda e, jj=j, nn=nj: e.matmul(
                        out=pA0[:, 0:NW],
                        lhsT=bcr1[:, jj * P:(jj + 1) * P],
                        rhs=w1T[:, jj * FF + nn * NW:jj * FF + nn * NW + NW],
                        start=(jj == 0), stop=(jj == DC - 1)),
                        inc=(j == DC - 1))
            else:
                for j in range(FC):
                    op("pe", lambda e, jj=j, nn=nj: e.matmul(
                        out=pA0[:, 0:NW],
                        lhsT=bcr2[:, jj * P:(jj + 1) * P],
                        rhs=w2T[:, jj * D + nn * NW:jj * D + nn * NW + NW],
                        start=(jj == 0), stop=(jj == FC - 1)),
                        inc=(j == FC - 1))
            ev(f"c.bmm.{bj}", "pe")
            w("dve", f"c.bmm.{bj}")
            w("dve", f"c.bld.{bj}")
            mcol = 2 if kind == "b1" else 3
            op("dve", lambda e, bb=bld, mc=mcol: e.scalar_tensor_tensor(
                out=bias_tmp[0:1, 0:NW], in0=pA0[0:1, 0:NW],
                scalar=sclb[0:1, mc:mc + 1], in1=bb[0:1, 0:NW],
                op0=ALU.mult, op1=ALU.add), inc=True)
            bm = bias1m if kind == "b1" else bias2m
            op("dve", lambda e, mm_=bm, nn=nj: e.tensor_copy(
                out=mm_[0:1, nn * NW:(nn + 1) * NW], in_=bias_tmp[0:1, 0:NW]), inc=True)
            op("dve", lambda e, mm_=bm, nn=nj: e.tensor_tensor(
                out=mm_[32:33, nn * NW:(nn + 1) * NW], in0=bias_tmp[0:1, 0:NW],
                in1=mm_[0:1, nn * NW:(nn + 1) * NW], op=ALU.subtract), inc=True)
            ev(f"c.bf.{bj}", "dve")
        op("dve", lambda e: e.tensor_scalar(out=g1c[:], in0=g1c[:],
                                            scalar1=sclb[:, 2:3], scalar2=None,
                                            op0=ALU.mult), inc=True)
        op("dve", lambda e: e.tensor_scalar(out=g2c[:], in0=g2c[:],
                                            scalar1=sclb[:, 3:4], scalar2=None,
                                            op0=ALU.mult), inc=True)
        for j in range(DC):
            op("dve", lambda e, jj=j: e.tensor_scalar(
                out=w1T[:, jj * FF:(jj + 1) * FF],
                in0=w1T[:, jj * FF:(jj + 1) * FF],
                scalar1=g1c[:, jj:jj + 1], scalar2=None, op0=ALU.mult),
                inc=(j == DC - 1))
        for j in range(FC):
            op("dve", lambda e, jj=j: e.tensor_scalar(
                out=w2T[:, jj * D:(jj + 1) * D],
                in0=w2T[:, jj * D:(jj + 1) * D],
                scalar1=g2c[:, jj:jj + 1], scalar2=None, op0=ALU.mult),
                inc=(j == FC - 1))
        ev("c.bias", "dve")

        def phase_c_tile(i):
            xbuf = cx0 if i % 2 == 0 else cx1
            if i >= 2:
                w("sp", f"c.out.{i - 2}")
            else:
                w("sp", f"b.res.{NT - 1}")
                w("sp", f"c.wT")
                w("sp", "c.bias")
            w("sp", f"b.x1.{i}")
            op("sp", lambda e: e.dma_start(out=xbuf[:],
                                           in_=x1d[i * P:(i + 1) * P, :]),
               inc=True, sem="sxc")
            ev(f"c.x.{i}", "sxc")

            w("dve", f"c.x.{i}")
            if i == 0:
                w("dve", "c.bias")
                w("dve", f"b.x1.{NT - 1}")
                w("dve", f"q2.store.{NQ - 1}")
            op("dve", lambda e: e.bn_stats(out=st6[:, 0:6], in_=xbuf[:, 0:512]), inc=True)
            op("dve", lambda e: e.bn_stats(out=st6[:, 6:12], in_=xbuf[:, 512:1024]), inc=True)
            dve_fence()
            op("dve", lambda e: e.bn_aggr(out=mv[:, 0:2], in_=st6[:, 0:12]), inc=True)
            dve_fence()
            op("dve", lambda e: e.tensor_scalar(out=nm[:], in0=mv[:, 0:1],
                                                scalar1=-1.0, scalar2=None,
                                                op0=ALU.mult), inc=True)
            newton_rsqrt(mv[:, 1:2], EPS_LN * EPS_LN, 1.0 + EPS_LN)
            if i > 0:
                w("dve", f"c.mm1.{i - 1}.7")
            op("dve", lambda e: e.tensor_scalar(out=zb[:], in0=xbuf[:],
                                                scalar1=nm[:, 0:1],
                                                scalar2=rs[:, 0:1],
                                                op0=ALU.add, op1=ALU.mult), inc=True)
            ev(f"c.z.{i}", "dve")

            w("pe", f"c.z.{i}")
            if i == 0:
                w("pe", "c.bias")
            if i > 0:
                w("act", f"c.mm1.{i - 1}.7")
            transpose_group(zb, i % 2, f"c.zT.{i}", zT[:], f"c.zT.evac.{i}")

            w("pe", f"c.zT.evac.{i}")
            for nj in range(8):
                pb = pA0 if nj % 2 == 0 else pA1
                if nj >= 2:
                    w("pe", f"c.gelu.{i}.{nj - 2}")
                elif i > 0:
                    w("pe", f"c.gelu.{i - 1}.{6 + nj}")
                for j in range(DC):
                    op("pe", lambda e, jj=j, nn=nj, pbb=pb: e.matmul(
                        out=pbb[:, 0:NW],
                        lhsT=zT[:, jj * P:(jj + 1) * P],
                        rhs=w1T[:, jj * FF + nn * NW:jj * FF + nn * NW + NW],
                        start=(jj == 0), stop=False))
                op("pe", lambda e, nn=nj, pbb=pb: e.matmul(
                    out=pbb[:, 0:NW],
                    lhsT=ones2[:, 0:P],
                    rhs=bias1m[:, nn * NW:(nn + 1) * NW],
                    start=False, stop=True), inc=True)
                ev(f"c.mm1.{i}.{nj}", "pe")
                w("act", f"c.mm1.{i}.{nj}")
                if nj == 0 and i > 0:
                    w("act", f"c.z2.{i - 1}.3")
                op("act", lambda e, nn=nj, pbb=pb: e.activation(
                    out=hbuf[:, nn * NW:(nn + 1) * NW], in_=pbb[:, 0:NW],
                    func=AF.Gelu), inc=True)
                ev(f"c.gelu.{i}.{nj}", "act")
                w("dve", f"c.gelu.{i}.{nj}")
                op("dve", lambda e, nn=nj: e.bn_stats(
                    out=st6b[:, nn * 6:(nn + 1) * 6],
                    in_=hbuf[:, nn * NW:(nn + 1) * NW]), inc=True)
                ev(f"c.hst.{i}.{nj}", "dve")

            dve_fence()
            op("dve", lambda e: e.bn_aggr(out=mv[:, 0:2], in_=st6b[:, 0:48]), inc=True)
            dve_fence()
            op("dve", lambda e: e.tensor_scalar(out=nm[:], in0=mv[:, 0:1],
                                                scalar1=-1.0, scalar2=None,
                                                op0=ALU.mult), inc=True)
            newton_rsqrt(mv[:, 1:2], EPS_LN, 1.0)

            for grp in range(4):
                if i > 0:
                    w("dve", f"c.z2T.{i - 1}.{grp}")
                op("dve", lambda e, gg=grp: e.tensor_scalar(
                    out=z2b[:, gg * 1024:(gg + 1) * 1024],
                    in0=hbuf[:, gg * 1024:(gg + 1) * 1024],
                    scalar1=nm[:, 0:1], scalar2=rs[:, 0:1],
                    op0=ALU.add, op1=ALU.mult), inc=True)
                ev(f"c.z2.{i}.{grp}", "dve")
            for grp in range(4):
                w("pe", f"c.z2.{i}.{grp}")
                if i > 0 and grp == 0:
                    w("act", f"c.mm2.{i - 1}")
                transpose_group(z2b[:, grp * 1024:(grp + 1) * 1024],
                                (i + grp) % 2, f"c.z2T.{i}.{grp}",
                                z2T[:, grp * 1024:(grp + 1) * 1024],
                                f"c.z2T.evac.{i}.{grp}")
            w("pe", f"c.z2T.evac.{i}.3")
            if i > 0:
                w("pe", f"c.out.{i - 1}")
            for nj in range(2):
                pb = pA2 if nj == 0 else pA3
                for j in range(FC):
                    op("pe", lambda e, jj=j, nn=nj, pbb=pb: e.matmul(
                        out=pbb[:, 0:NW],
                        lhsT=z2T[:, jj * P:(jj + 1) * P],
                        rhs=w2T[:, jj * D + nn * NW:jj * D + nn * NW + NW],
                        start=(jj == 0), stop=False))
                op("pe", lambda e, nn=nj, pbb=pb: e.matmul(
                    out=pbb[:, 0:NW],
                    lhsT=ones2[:, 0:P],
                    rhs=bias2m[:, nn * NW:(nn + 1) * NW],
                    start=False, stop=True), inc=(nj == 1))
            ev(f"c.mm2.{i}", "pe")

            w("dve", f"c.mm2.{i}")
            if i > 0:
                w("dve", f"c.y.{i - 1}")
            for half in range(2):
                pb = pA2 if half == 0 else pA3
                s = half * NW
                op("dve", lambda e, pbb=pb, ss=s: e.tensor_tensor(
                    out=outf[:, ss:ss + NW], in0=pbb[:, 0:NW],
                    in1=xbuf[:, ss:ss + NW], op=ALU.add), inc=(half == 1))
            ev(f"c.out.{i}", "dve")
            w("sp", f"c.out.{i}")
            op("sp", lambda e: e.dma_start(out=y[i * P:(i + 1) * P, :],
                                           in_=outf[:]), inc=True, sem="sout")
            ev(f"c.y.{i}", "sout")

        for i in range(NT):
            phase_c_tile(i)

        if PHASE == "C5":
            w("pool", f"c.y.{NT - 1}")
            op("pool", lambda e: e.dma_start(out=y[0:P, :], in_=wq_s[:, 0:D]), inc=True, sem="sout")
            ev("dbg.c5a", "sout")
            w("pool", "dbg.c5a")
            op("pool", lambda e: e.dma_start(allow_slow_non_contiguous=True,
                                             out=y[0:P, 1020:1024], in_=sclb[:]),
               inc=True, sem="sout")
            ev("dbg.c5b", "sout")
            w("pool", "dbg.c5b")
            op("pool", lambda e: e.dma_start(allow_slow_non_contiguous=True,
                                             out=y[0:P, 940:1006], in_=absacc[:]),
               inc=True, sem="sout")
            op("pool", lambda e: e.dma_start(out=y[P:2 * P, :], in_=hbuf[:, 0:D]), inc=True, sem="sout")
            op("pool", lambda e: e.dma_start(out=y[2 * P:3 * P, :], in_=z2b[:, 0:D]), inc=True, sem="sout")
            op("pool", lambda e: e.dma_start(out=y[3 * P:4 * P, :], in_=outf[:]), inc=True, sem="sout")
            ev("dbg.c5", "sout")
            w("sp", "dbg.c5")

        if PHASE == "C4":
            w("pool", f"c.y.{NT - 1}")
            op("pool", lambda e: e.dma_start(
                out=y[0:P, :], in_=bias2m[:, 0:D]), inc=True, sem="sout")
            op("pool", lambda e: e.dma_start(
                out=y[P:2 * P, 0:D], in_=bias1m[:, 0:D]), inc=True, sem="sout")
            ev("dbg.c4", "sout")
            w("sp", "dbg.c4")

        if PHASE == "C3":
            w("pool", f"c.y.{NT - 1}")
            for g in range(2):
                op("pool", lambda e, gg=g: e.dma_start(
                    out=y[gg * P:(gg + 1) * P, :], in_=z2b[:, (2 + gg) * D:(3 + gg) * D]),
                    inc=True, sem="sout")
            for g in range(2):
                op("pool", lambda e, gg=g: e.dma_start(
                    out=y[(2 + gg) * P:(3 + gg) * P, :], in_=z2T[:, (2 + gg) * D:(3 + gg) * D]),
                    inc=True, sem="sout")
            ev("dbg.c3", "sout")
            w("sp", "dbg.c3")

        if PHASE == "C2":
            w("pool", f"c.y.{NT - 1}")
            for g in range(4):
                op("pool", lambda e, gg=g: e.dma_start(
                    out=y[gg * P:(gg + 1) * P, :], in_=z2b[:, gg * D:(gg + 1) * D]),
                    inc=True, sem="sout")
            ev("dbg.c2", "sout")
            w("sp", "dbg.c2")

        if PHASE == "C1":
            w("pool", f"c.y.{NT - 1}")
            op("pool", lambda e: e.dma_start(out=y[0:P, :], in_=zb[:]),
               inc=True, sem="sout")
            op("pool", lambda e: e.dma_start(out=y[P:2 * P, :], in_=hbuf[:, 0:D]),
               inc=True, sem="sout")
            op("pool", lambda e: e.dma_start(out=y[2 * P:3 * P, :], in_=z2b[:, 0:D]),
               inc=True, sem="sout")
            op("pool", lambda e: e.dma_start(out=y[3 * P:4 * P, 0:D], in_=w1T[:, 0:D]),
               inc=True, sem="sout")
            op("pool", lambda e: e.dma_start(allow_slow_non_contiguous=True,
                                             out=y[0:P, 1016:1020], in_=sclb[:]),
               inc=True, sem="sout")
            ev("dbg.c", "sout")
            w("sp", "dbg.c")

        w("sp", f"c.y.{NT - 1}")

    eng_map = {}

    def run_section(key, handle):
        eng_map[key] = handle
        E = Emitter(nc, eng_map, sems, key)
        emit_program(E)

    @block.tensor
    def _(tensor):
        run_section("pe", tensor)

    @block.scalar
    def _(scalar):
        run_section("act", scalar)

    @block.vector
    def _(vector):
        run_section("dve", vector)

    @block.gpsimd
    def _(gpsimd):
        run_section("pool", gpsimd)

    @block.sync
    def _(sync):
        run_section("sp", sync)

    stack.close()
    return nc


_NC_CACHE = {}


def kernel(L0_attn_x, L0_attn_Wq, L0_attn_Wk, L0_attn_Wv, L0_attn_Wo,
           L0_ffn_ln1_g, L0_ffn_ln1_b, L0_ffn_w1, L0_ffn_b1,
           L0_ffn_ln2_g, L0_ffn_ln2_b, L0_ffn_w2, L0_ffn_b2):
    x = np.ascontiguousarray(np.asarray(L0_attn_x, np.float32).reshape(2 * 16384, D))
    nonce = int(np.random.randint(1, 2**30))
    ident = np.eye(P, dtype=ml_dtypes.bfloat16)
    common = {
        "wq": np.ascontiguousarray(np.asarray(L0_attn_Wq, np.float32)),
        "wk": np.ascontiguousarray(np.asarray(L0_attn_Wk, np.float32)),
        "wv": np.ascontiguousarray(np.asarray(L0_attn_Wv, np.float32)),
        "wo": np.ascontiguousarray(np.asarray(L0_attn_Wo, np.float32)),
        "w1": np.ascontiguousarray(np.asarray(L0_ffn_w1, np.float32)),
        "w2": np.ascontiguousarray(np.asarray(L0_ffn_w2, np.float32)),
        "identd": ident,
        "g1cols": np.ascontiguousarray(np.asarray(L0_ffn_ln1_g, np.float32).reshape(DC, P).T),
        "b1lncols": np.ascontiguousarray(np.asarray(L0_ffn_ln1_b, np.float32).reshape(DC, P).T),
        "g2cols": np.ascontiguousarray(np.asarray(L0_ffn_ln2_g, np.float32).reshape(FC, P).T),
        "b2lncols": np.ascontiguousarray(np.asarray(L0_ffn_ln2_b, np.float32).reshape(FC, P).T),
        "b1row": np.asarray(L0_ffn_b1, np.float32).reshape(1, FF),
        "b2row": np.asarray(L0_ffn_b2, np.float32).reshape(1, D),
    }
    in_maps = []
    for c in range(8):
        mirror = c ^ 2
        m = dict(common)
        m["xo"] = np.ascontiguousarray(x[c * TPC:(c + 1) * TPC])
        m["xf"] = np.ascontiguousarray(x[mirror * TPC:(mirror + 1) * TPC])
        m["meta"] = np.array([[c % 2, nonce, 0, 0, 0, 0, 0, 0]], np.int32)
        in_maps.append(m)

    if "v1" not in _NC_CACHE:
        _NC_CACHE["v1"] = build_nc()
    nc = _NC_CACHE["v1"]
    res = run_bass_kernel_spmd(nc, in_maps, core_ids=list(range(8)))
    out = np.concatenate([res.results[c]["y"] for c in range(8)], 0)
    return out.reshape(2, 16384, D)


if __name__ == "__main__":
    nc = build_nc()
    n = sum(len(b.instructions) for f in nc.m.functions for b in f.blocks)
    print("built ok,", n, "instructions")
